# version 06
# baseline (speedup 1.0000x reference)
import numpy as np
import concourse.bass as bass
import concourse.mybir as mybir
from concourse.bass_utils import run_bass_kernel_spmd

F32 = mybir.dt.float32
BF16 = mybir.dt.bfloat16
ALU = mybir.AluOpType
AF = mybir.ActivationFunctionType
AX = mybir.AxisListType

B = 2
T = 8192
D = 2048
DQ = 512
NST = 16
RK = 128
C = 1024
NCH = T // C
NTT = C // 128
EPS = 1e-5
RG = [[0, 1, 2, 3], [4, 5, 6, 7]]

ENGINES = ("sync", "gpsimd", "vector", "scalar", "tensor")

YA_ON_GP = True
DEBUG_STOP = None


class _StopBuild(Exception):
    pass


class Tape:

    def __init__(self):
        self.q = {e: [] for e in ENGINES}
        self.cnt = {}

    def op(self, eng, fn):
        self.q[eng].append([fn, None, 0])

    def mark(self, eng, sem, amt=1):
        ent = self.q[eng][-1]
        assert ent[1] is None, "op already carries an inc"
        self.cnt[sem] = self.cnt.get(sem, 0) + amt
        ent[1], ent[2] = sem, amt
        return self.cnt[sem]

    def inc(self, eng, sem, fn, amt):
        self.op(eng, fn)
        return self.mark(eng, sem, amt)

    def wait(self, eng, sem, val):
        if val is None or val <= 0:
            return
        self.q[eng].append([("wait", sem, val), None, 0])

    def cur(self, sem):
        return self.cnt.get(sem, 0)

    def replay(self, eng, E, sems):
        for fn, sem, amt in self.q[eng]:
            if isinstance(fn, tuple):
                E.wait_ge(sems[fn[1]], fn[2])
                continue
            inst = fn(E)
            if sem is not None:
                assert inst is not None, f"{eng} op with inc returned no inst"
                inst.then_inc(sems[sem], amt)


def build():
    nc = bass.Bass("TRN2", target_bir_lowering=False)

    xT_in = nc.dram_tensor("xT_in", [T, DQ], F32, kind="ExternalInput").ap()
    wx_in = nc.dram_tensor("wx_in", [DQ, 160], F32, kind="ExternalInput").ap()
    wdt_in = nc.dram_tensor("wdt_in", [RK, DQ], F32, kind="ExternalInput").ap()
    bdt_in = nc.dram_tensor("bdt_in", [128, 4], F32, kind="ExternalInput").ap()
    alog_in = nc.dram_tensor("alog_in", [128, 64], F32, kind="ExternalInput").ap()
    dpar_in = nc.dram_tensor("dpar_in", [128, 4], F32, kind="ExternalInput").ap()
    wup_in = nc.dram_tensor("wup_in", [D, DQ], F32, kind="ExternalInput").ap()
    wdn_in = nc.dram_tensor("wdn_in", [D, DQ], F32, kind="ExternalInput").ap()
    wex_in = nc.dram_tensor("wex_in", [D, DQ], F32, kind="ExternalInput").ap()
    ident_in = nc.dram_tensor("ident_in", [128, 128], F32, kind="ExternalInput").ap()
    sel32_in = nc.dram_tensor("sel32_in", [32, 32 * 128], F32, kind="ExternalInput").ap()
    out_hbm = nc.dram_tensor("outT", [T, DQ], F32, kind="ExternalOutput").ap()

    cc0_in = nc.dram_tensor("cc0_in", [128, 16 * NCH], F32, kind="Internal")
    cc0_out = nc.dram_tensor("cc0_out", [128, 16 * NCH], F32, kind="Internal")
    ccxz_in = nc.dram_tensor("ccxz_in", [160, T], F32, kind="Internal")
    ccxz_out = nc.dram_tensor("ccxz_out", [160, T], F32, kind="Internal")
    cc1_in = nc.dram_tensor("cc1_in", [128, 24 * NCH], F32, kind="Internal")
    cc1_out = nc.dram_tensor("cc1_out", [128, 24 * NCH], F32, kind="Internal")
    cc3_in = nc.dram_tensor("cc3_in", [128, 16 * NCH], F32, kind="Internal")
    cc3_out = nc.dram_tensor("cc3_out", [128, 16 * NCH], F32, kind="Internal")
    agx_in = nc.dram_tensor("agx_in", [DQ, T], BF16, kind="Internal")
    agx_out = nc.dram_tensor("agx_out", [D, T], BF16, kind="Internal")
    agu_in = nc.dram_tensor("agu_in", [DQ, T], BF16, kind="Internal")
    agu_out = nc.dram_tensor("agu_out", [D, T], BF16, kind="Internal")
    ybuf = nc.dram_tensor("ybuf", [T, DQ], F32, kind="Internal")
    x1buf = nc.dram_tensor("x1buf", [T, DQ], F32, kind="Internal")
    fbuf = nc.dram_tensor("fbuf", [T, DQ], F32, kind="Internal")

    xT_tiles = xT_in.rearrange("(n p) d -> n p d", p=128)
    yb_tiles = ybuf.ap().rearrange("(n p) d -> n p d", p=128)
    x1_tiles = x1buf.ap().rearrange("(n p) d -> n p d", p=128)
    fb_tiles = fbuf.ap().rearrange("(n p) d -> n p d", p=128)
    out_tiles = out_hbm.rearrange("(n p) d -> n p d", p=128)

    tp = Tape()

    from contextlib import ExitStack
    with ExitStack() as ctx:
        wup_bf = ctx.enter_context(nc.sbuf_tensor("wup_bf", [128, 16 * DQ], BF16))
        wdn_bf = ctx.enter_context(nc.sbuf_tensor("wdn_bf", [128, 16 * DQ], BF16))
        wex_bf = ctx.enter_context(nc.sbuf_tensor("wex_bf", [128, 16 * DQ], BF16))
        wx_bf = ctx.enter_context(nc.sbuf_tensor("wx_bf", [128, 4 * 160], BF16))
        wdt_bf = ctx.enter_context(nc.sbuf_tensor("wdt_bf", [128, DQ], BF16))
        bdt_t = ctx.enter_context(nc.sbuf_tensor("bdt_t", [128, 4], F32))
        aneg = ctx.enter_context(nc.sbuf_tensor("aneg", [128, 64], F32))
        dpar_t = ctx.enter_context(nc.sbuf_tensor("dpar_t", [128, 4], F32))
        ident_f = ctx.enter_context(nc.sbuf_tensor("ident_f", [128, 128], F32))
        ident_b = ctx.enter_context(nc.sbuf_tensor("ident_b", [128, 128], BF16))
        sel32 = ctx.enter_context(nc.sbuf_tensor("sel32", [32, 32 * 128], BF16))
        st0 = ctx.enter_context(nc.sbuf_tensor("st0", [128, 16 * NCH], F32))
        mu0 = ctx.enter_context(nc.sbuf_tensor("mu0", [128, 8 * NCH], F32))
        r0c = ctx.enter_context(nc.sbuf_tensor("r0c", [128, 8 * NCH], F32))
        st1 = ctx.enter_context(nc.sbuf_tensor("st1", [128, 24 * NCH], F32))
        st3 = ctx.enter_context(nc.sbuf_tensor("st3", [128, 16 * NCH], F32))
        lncol = ctx.enter_context(nc.sbuf_tensor("lncol", [128, 64], F32))
        xT_t = ctx.enter_context(nc.sbuf_tensor("xT_t", [128, NTT * DQ], F32))
        scr = ctx.enter_context(nc.sbuf_tensor("scr", [128, NTT * DQ], F32))
        pool = ctx.enter_context(nc.sbuf_tensor("pool", [128, 4 * C], F32))
        dt_f = ctx.enter_context(nc.sbuf_tensor("dt_f", [128, 4 * C], F32))
        u_f = ctx.enter_context(nc.sbuf_tensor("u_f", [128, 4 * C], F32))
        ya_f = ctx.enter_context(nc.sbuf_tensor("ya_f", [128, 4 * C], F32))
        da_sl = ctx.enter_context(nc.sbuf_tensor("da_sl", [128, 2 * C], F32))
        dbx_sl = ctx.enter_context(nc.sbuf_tensor("dbx_sl", [128, 2 * C], F32))
        sm_scr = ctx.enter_context(nc.sbuf_tensor("sm_scr", [128, 512], F32))
        dtraw_b = ctx.enter_context(nc.sbuf_tensor("dtraw_b", [128, C], BF16))
        bc_t = ctx.enter_context(nc.sbuf_tensor("bc_t", [32, C], F32))
        bc_h = ctx.enter_context(nc.sbuf_tensor("bc_h", [32, C], BF16))
        bc_l = ctx.enter_context(nc.sbuf_tensor("bc_l", [32, C], BF16))
        bc_l2 = ctx.enter_context(nc.sbuf_tensor("bc_l2", [32, C], BF16))
        xz32 = ctx.enter_context(nc.sbuf_tensor("xz32", [32, C], F32))
        x1n_b = ctx.enter_context(nc.sbuf_tensor("x1n_b", [128, 4 * C], BF16))
        up_b = ctx.enter_context(nc.sbuf_tensor("up_b", [128, 4 * C], BF16))
        psA = ctx.enter_context(nc.psum_tensor("psA", [128, 1024], F32))
        psB = ctx.enter_context(nc.psum_tensor("psB", [128, 1024], F32))
        psM = ctx.enter_context(nc.psum_tensor("psM", [128, 512], F32))
        psN = ctx.enter_context(nc.psum_tensor("psN", [128, 512], F32))
        psT = ctx.enter_context(nc.psum_tensor("psT", [128, 512], F32))
        psX = ctx.enter_context(nc.psum_tensor("psX", [128, 1024], BF16))
        SEM_ld = ctx.enter_context(nc.semaphore("ld"))
        SEM_st = ctx.enter_context(nc.semaphore("st"))
        SEM_cc = ctx.enter_context(nc.semaphore("cc"))
        SEM_pe = ctx.enter_context(nc.semaphore("pe"))
        SEM_dv = ctx.enter_context(nc.semaphore("dv"))
        SEM_ac = ctx.enter_context(nc.semaphore("ac"))
        SEM_gp = ctx.enter_context(nc.semaphore("gp"))
        SEM_gl = ctx.enter_context(nc.semaphore("gl"))
        block = ctx.enter_context(nc.Block())
        SEMS = {"ld": SEM_ld, "st": SEM_st, "cc": SEM_cc, "pe": SEM_pe,
                "dv": SEM_dv, "ac": SEM_ac, "gp": SEM_gp, "gl": SEM_gl}

        hstate = lncol
        out_flat = out_hbm.rearrange("a b -> (a b)")
        xT_chunks = xT_in.rearrange("(k a p) d -> k p a d", p=128, a=NTT)
        yb_chunks = ybuf.ap().rearrange("(k a p) d -> k p a d", p=128, a=NTT)
        x1_chunks = x1buf.ap().rearrange("(k a p) d -> k p a d", p=128, a=NTT)
        fb_chunks = fbuf.ap().rearrange("(k a p) d -> k p a d", p=128, a=NTT)
        xT_v = xT_t[:].rearrange("p (a d) -> p a d", a=NTT)
        scr_v = scr[:].rearrange("p (a d) -> p a d", a=NTT)

        def maybe_stop(tag, bufs):
            if DEBUG_STOP != tag:
                return
            off = 0
            for t in bufs:
                n = int(np.prod(t.shape))
                gstore(out_flat[off:off + n], t.ap().rearrange("a b -> (a b)"))
                off += n
            tp.wait("gpsimd", "st", tp.cur("st"))
            raise _StopBuild

        xn_f = pool
        h_sl = pool[:, 0:2 * C]
        hc_sl = pool[:, 2 * C:4 * C]
        kt_a = x1n_b[:, 0:2 * C]
        kt_b = x1n_b[:, 2 * C:4 * C]

        def sload(dst, src):
            return tp.inc("sync", "ld",
                          lambda E, d=dst, s=src: E.dma_start(out=d, in_=s), 16)

        def gstore(dst, src):
            return tp.inc("gpsimd", "st",
                          lambda E, d=dst, s=src: E.dma_start(out=d, in_=s), 16)

        def collective(kind, op, cin, cout):
            return tp.inc("gpsimd", "cc",
                          lambda E, k=kind, o=op, a=cin, b=cout:
                          E.collective_compute(k, o, ins=[a.ap()], outs=[b.ap()],
                                               replica_groups=RG), 1)

        def v_op(fn):
            tp.op("vector", fn)

        def v_mark(fn=None):
            if fn is not None:
                tp.op("vector", fn)
            return tp.mark("vector", "dv")

        def vseq(*fns):
            mv = None
            for fn in fns:
                tp.op("vector", fn)
                mv = tp.mark("vector", "dv")
                tp.wait("vector", "dv", mv)
            return mv

        def s_op(fn):
            tp.op("scalar", fn)

        def s_mark(fn=None):
            if fn is not None:
                tp.op("scalar", fn)
            return tp.mark("scalar", "ac")

        def p_op(fn):
            tp.op("tensor", fn)

        def p_mark(fn=None):
            if fn is not None:
                tp.op("tensor", fn)
            return tp.mark("tensor", "pe")

        def g_mark(fn):
            tp.op("gpsimd", fn)
            return tp.mark("gpsimd", "gp")

        def wv(eng, val):
            tp.wait(eng, "dv", val)

        def wa(eng, val):
            tp.wait(eng, "ac", val)

        def wp(eng, val):
            tp.wait(eng, "pe", val)

        def wg(eng, val):
            tp.wait(eng, "gp", val)

        def wl(eng):
            tp.wait(eng, "ld", tp.cur("ld"))

        def wst(eng):
            tp.wait(eng, "st", tp.cur("st"))

        def wc(eng, val):
            tp.wait(eng, "cc", val)

        def emit():
            last_cast = 0
            for wi, (src, dstb) in enumerate(
                    [(wup_in, wup_bf), (wdn_in, wdn_bf), (wex_in, wex_bf)]):
                for g4 in range(4):
                    if last_cast:
                        tp.wait("sync", "dv", last_cast)
                    for j in range(4):
                        kt = g4 * 4 + j
                        sload(scr[:, j * DQ:(j + 1) * DQ],
                              src[kt * 128:(kt + 1) * 128, :])
                    wl("vector")
                    for j in range(4):
                        kt = g4 * 4 + j
                        v_op(lambda E, j=j, kt=kt, dstb=dstb: E.tensor_copy(
                            dstb[:, kt * DQ:(kt + 1) * DQ],
                            scr[:, j * DQ:(j + 1) * DQ]))
                    last_cast = v_mark()
            tp.wait("sync", "dv", last_cast)
            for j in range(4):
                sload(scr[:, j * 160:(j + 1) * 160], wx_in[j * 128:(j + 1) * 128, :])
            sload(scr[:, 640:1152], wdt_in)
            sload(bdt_t[:], bdt_in)
            sload(scr[:, 1152:1216], alog_in)
            sload(dpar_t[:], dpar_in)
            sload(ident_f[:], ident_in)
            sload(xT_t[0:32, 0:4096], sel32_in)
            wl("vector")
            for j in range(4):
                v_op(lambda E, j=j: E.tensor_copy(
                    wx_bf[:, j * 160:(j + 1) * 160], scr[:, j * 160:(j + 1) * 160]))
            v_op(lambda E: E.tensor_copy(wdt_bf[:], scr[:, 640:1152]))
            v_op(lambda E: E.tensor_copy(ident_b[:], ident_f[:]))
            v_op(lambda E: E.tensor_copy(sel32[:], xT_t[0:32, 0:4096]))
            mv0 = v_mark()
            wl("scalar")
            ma = s_mark(lambda E: E.activation(out=aneg[:], in_=scr[:, 1152:1216],
                                               func=AF.Exp))
            wa("vector", ma)
            v_op(lambda E: E.tensor_scalar_mul(aneg[:], aneg[:], -1.0))
            P0_dv = v_mark()

            for k in range(NCH):
                tp.wait("sync", "dv", tp.cur("dv"))
                tp.wait("sync", "ac", tp.cur("ac"))
                sload(xT_v, xT_chunks[k])
                wl("vector")
                for tt in range(NTT):
                    v_op(lambda E, k=k, tt=tt: E.reduce_sum(
                        st0[:, k * 16 + tt:k * 16 + tt + 1],
                        xT_t[:, tt * DQ:(tt + 1) * DQ], axis=AX.X))
                v_mark()
                wl("scalar")
                for tt in range(NTT):
                    s_op(lambda E, k=k, tt=tt: E.activation(
                        out=sm_scr[:, 0:512],
                        in_=xT_t[:, tt * DQ:(tt + 1) * DQ], func=AF.Square,
                        accum_out=st0[:, k * 16 + 8 + tt:k * 16 + 9 + tt]))
                s_mark()
            wv("gpsimd", tp.cur("dv"))
            wa("gpsimd", tp.cur("ac"))
            gstore(cc0_in.ap(), st0[:])
            wst("gpsimd")
            cc_v = collective("AllReduce", ALU.add, cc0_in, cc0_out)
            wc("sync", cc_v)
            sload(st0[:], cc0_out.ap())
            wl("vector")
            for k in range(NCH):
                mv = vseq(
                    lambda E, k=k: E.tensor_scalar_mul(
                        mu0[:, k * 8:(k + 1) * 8], st0[:, k * 16:k * 16 + 8],
                        1.0 / D),
                    lambda E, k=k: E.tensor_scalar_mul(
                        r0c[:, k * 8:(k + 1) * 8], st0[:, k * 16 + 8:k * 16 + 16],
                        1.0 / D),
                    lambda E, k=k: E.tensor_tensor(
                        out=lncol[:, 0:8], in0=mu0[:, k * 8:(k + 1) * 8],
                        in1=mu0[:, k * 8:(k + 1) * 8], op=ALU.mult),
                    lambda E, k=k: E.tensor_tensor(
                        out=r0c[:, k * 8:(k + 1) * 8],
                        in0=r0c[:, k * 8:(k + 1) * 8],
                        in1=lncol[:, 0:8], op=ALU.subtract),
                    lambda E, k=k: E.tensor_scalar_add(
                        r0c[:, k * 8:(k + 1) * 8], r0c[:, k * 8:(k + 1) * 8], EPS),
                )
            wv("scalar", mv)
            ma = s_mark(lambda E: E.activation(out=r0c[:], in_=r0c[:], func=AF.Sqrt))
            wa("vector", ma)
            mv = tp.inc("vector", "dv", lambda E: E.reciprocal(r0c[:], r0c[:]), 1)
            tp.wait("vector", "dv", mv)

            for k in range(NCH):
                tp.wait("sync", "dv", tp.cur("dv"))
                tp.wait("sync", "ac", tp.cur("ac"))
                sload(xT_v, xT_chunks[k])
                wl("vector")
                for tt in range(NTT):
                    v_op(lambda E, k=k, tt=tt: E.tensor_scalar(
                        out=x1n_b[:, tt * DQ:(tt + 1) * DQ],
                        in0=xT_t[:, tt * DQ:(tt + 1) * DQ],
                        scalar1=mu0[:, k * 8 + tt:k * 8 + tt + 1],
                        scalar2=r0c[:, k * 8 + tt:k * 8 + tt + 1],
                        op0=ALU.subtract, op1=ALU.mult))
                mv = v_mark()
                wv("tensor", mv)
                for dti in range(4):
                    for tt in range(NTT):
                        p_op(lambda E, dti=dti, tt=tt: E.transpose(
                            out=psX[:, tt * 128:(tt + 1) * 128],
                            in_=x1n_b[:, tt * DQ + dti * 128:tt * DQ + dti * 128 + 128],
                            identity=ident_b[:]))
                    mp = p_mark()
                    wp("scalar", mp)
                    ma = s_mark(lambda E, dti=dti: E.activation(
                        out=up_b[:, dti * C:(dti + 1) * C], in_=psX[:, 0:1024],
                        func=AF.Copy))
                    wa("tensor", ma)
                for nb in range(2):
                    def mm_xz(E, nb=nb):
                        for kt in range(4):
                            E.matmul(out=psM[:, 0:512],
                                     lhsT=wx_bf[:, kt * 160:kt * 160 + 128],
                                     rhs=up_b[:, kt * C + nb * 512:kt * C + nb * 512 + 512],
                                     start=(kt == 0), stop=(kt == 3))
                        for kt in range(4):
                            last = E.matmul(out=psN[0:32, 0:512],
                                     lhsT=wx_bf[:, kt * 160 + 128:kt * 160 + 160],
                                     rhs=up_b[:, kt * C + nb * 512:kt * C + nb * 512 + 512],
                                     start=(kt == 0), stop=(kt == 3))
                        return last
                    mp = p_mark(mm_xz)
                    wp("vector", mp)
                    v_op(lambda E, nb=nb: E.tensor_copy(
                        scr[:, nb * 512:(nb + 1) * 512], psM[:, 0:512]))
                    v_op(lambda E, nb=nb: E.tensor_copy(
                        xz32[:, nb * 512:(nb + 1) * 512], psN[0:32, 0:512]))
                    mv = v_mark()
                    wv("tensor", mv)
                    wv("gpsimd", mv)
                    gstore(ccxz_in.ap()[0:128, k * C + nb * 512:k * C + nb * 512 + 512],
                           scr[:, nb * 512:(nb + 1) * 512])
                gstore(ccxz_in.ap()[128:160, k * C:(k + 1) * C], xz32[:])
            wst("gpsimd")
            cc_xz = collective("AllReduce", ALU.add, ccxz_in, ccxz_out)
            if DEBUG_STOP == "P1b":
                tp.wait("gpsimd", "cc", cc_xz)
                maybe_stop("P1b", [ccxz_out, cc0_out])

            v_op(lambda E: E.memset(hstate[:], 0.0))
            v_mark()
            da_gate = {}
            for k in range(NCH):
                git0 = k * NST * 4
                tp.wait("sync", "dv", tp.cur("dv"))
                tp.wait("sync", "ac", tp.cur("ac"))
                tp.wait("sync", "gp", git0)
                if k == 0:
                    tp.wait("sync", "cc", cc_xz)
                sload(xT_v, xT_chunks[k])
                sload(dbx_sl[:, 0:C], ccxz_out.ap()[0:128, k * C:(k + 1) * C])
                sload(bc_t[:], ccxz_out.ap()[128:160, k * C:(k + 1) * C])
                wl("vector")
                if k == 0:
                    wc("vector", cc_xz)
                for tt in range(NTT):
                    v_op(lambda E, k=k, tt=tt: E.tensor_scalar(
                        out=scr[:, tt * DQ:(tt + 1) * DQ],
                        in0=xT_t[:, tt * DQ:(tt + 1) * DQ],
                        scalar1=mu0[:, k * 8 + tt:k * 8 + tt + 1],
                        scalar2=r0c[:, k * 8 + tt:k * 8 + tt + 1],
                        op0=ALU.subtract, op1=ALU.mult))
                v_op(lambda E: E.tensor_copy(bc_h[:], bc_t[:]))
                v_op(lambda E: E.tensor_copy(xz32[:], bc_h[:]))
                v_op(lambda E: E.tensor_tensor(out=xz32[:], in0=bc_t[:], in1=xz32[:],
                                               op=ALU.subtract))
                v_op(lambda E: E.tensor_copy(bc_l[:], xz32[:]))
                v_op(lambda E: E.tensor_copy(bc_t[:], bc_l[:]))
                v_op(lambda E: E.tensor_tensor(out=xz32[:], in0=xz32[:], in1=bc_t[:],
                                               op=ALU.subtract))
                v_op(lambda E: E.tensor_copy(bc_l2[:], xz32[:]))
                v_op(lambda E: E.tensor_copy(dtraw_b[:], dbx_sl[:, 0:C]))
                mv_pre = v_mark()
                wv("tensor", mv_pre)
                wg("scalar", git0)
                for dti in range(4):
                    for g in range(2):
                        for j in range(4):
                            tt = g * 4 + j
                            p_op(lambda E, dti=dti, tt=tt, j=j: E.transpose(
                                out=psT[:, j * 128:(j + 1) * 128],
                                in_=scr[:, tt * DQ + dti * 128:tt * DQ + dti * 128 + 128],
                                identity=ident_f[:]))
                        mp = p_mark()
                        wp("scalar", mp)
                        ma = s_mark(lambda E, dti=dti, g=g: E.activation(
                            out=xn_f[:, dti * C + g * 512:dti * C + g * 512 + 512],
                            in_=psT[:, 0:512], func=AF.Copy))
                        wa("tensor", ma)
                for dti in range(4):
                    for nb in range(2):
                        mp = p_mark(lambda E, dti=dti, nb=nb: E.matmul(
                            out=psM[:, 0:512],
                            lhsT=wdt_bf[:, dti * 128:dti * 128 + 128],
                            rhs=dtraw_b[:, nb * 512:(nb + 1) * 512],
                            start=True, stop=True))
                        wp("scalar", mp)

                        def sp1(E, dti=dti, nb=nb):
                            col = bdt_t[:, dti:dti + 1]
                            seg = dt_f[:, dti * C + nb * 512:dti * C + nb * 512 + 512]
                            E.activation(out=seg, in_=psM[:, 0:512], func=AF.Relu,
                                         bias=col)
                            return E.activation(out=sm_scr[:, 0:512],
                                                in_=psM[:, 0:512],
                                                func=AF.Abs, bias=col)
                        ma = s_mark(sp1)
                        tp.wait("scalar", "ac", ma)
                        ma = s_mark(lambda E: E.activation(
                            out=sm_scr[:, 0:512], in_=sm_scr[:, 0:512],
                            func=AF.Exp, scale=-1.0))
                        tp.wait("scalar", "ac", ma)
                        ma = s_mark(lambda E: E.activation(
                            out=sm_scr[:, 0:512], in_=sm_scr[:, 0:512],
                            func=AF.Ln, bias=1.0))
                        wa("tensor", ma)
                        wa("vector", ma)
                        mv = v_mark(lambda E, dti=dti, nb=nb: E.tensor_tensor(
                            out=dt_f[:, dti * C + nb * 512:dti * C + nb * 512 + 512],
                            in0=dt_f[:, dti * C + nb * 512:dti * C + nb * 512 + 512],
                            in1=sm_scr[:, 0:512], op=ALU.add))
                        wv("scalar", mv)
                if k > 0:
                    wg("vector", git0)
                    wp("vector", tp.cur("pe"))
                for dti in range(4):
                    v_op(lambda E, dti=dti: E.tensor_tensor(
                        out=u_f[:, dti * C:(dti + 1) * C],
                        in0=dt_f[:, dti * C:(dti + 1) * C],
                        in1=xn_f[:, dti * C:(dti + 1) * C], op=ALU.mult))
                    v_op(lambda E, dti=dti: E.tensor_scalar(
                        out=ya_f[:, dti * C:(dti + 1) * C],
                        in0=xn_f[:, dti * C:(dti + 1) * C],
                        scalar1=dpar_t[:, dti:dti + 1], scalar2=None, op0=ALU.mult))
                mv_u = v_mark()
                wv("scalar", mv_u)
                if DEBUG_STOP == "P2h" and k == 1:
                    tp.op("vector", lambda E: E.tensor_copy(
                        dbx_sl[:, C:2 * C], dtraw_b[:]))
                    mv_dbg = tp.mark("vector", "dv")
                    wv("gpsimd", mv_dbg)
                    gstore(out_flat[0:128 * 64].rearrange("(a b) -> a b", a=128),
                           hstate[:])
                    gstore(out_flat[128 * 64:128 * 64 + 128 * 4096].rearrange(
                        "(a b) -> a b", a=128), dt_f[:])
                    gstore(out_flat[128 * 64 + 128 * 4096:
                                    128 * 64 + 128 * 4096 + 128 * 1024].rearrange(
                        "(a b) -> a b", a=128), dbx_sl[:, C:2 * C])
                    tp.wait("gpsimd", "st", tp.cur("st"))
                    raise _StopBuild

                c_sb = scr
                ev_gate = {}
                pend_copy = [None]
                for n in range(NST):
                    def rep(E, n=n):
                        for nb in range(2):
                            for i, src in enumerate((bc_h, bc_l, bc_l2)):
                                E.matmul(out=psA[:, nb * 512:(nb + 1) * 512],
                                         lhsT=sel32[:, n * 128:(n + 1) * 128],
                                         rhs=src[:, nb * 512:(nb + 1) * 512],
                                         start=(i == 0), stop=(i == 2))
                            for i, src in enumerate((bc_h, bc_l, bc_l2)):
                                last = E.matmul(out=psB[:, nb * 512:(nb + 1) * 512],
                                         lhsT=sel32[:, (16 + n) * 128:(17 + n) * 128],
                                         rhs=src[:, nb * 512:(nb + 1) * 512],
                                         start=(i == 0), stop=(i == 2))
                        return last
                    if n == 0:
                        wv("tensor", mv_u)
                    else:
                        wv("tensor", tp.cur("dv"))
                        wg("tensor", git0 + n * 4)
                        wa("tensor", ev_gate[n - 1])
                    mp_rep = p_mark(rep)
                    wp("scalar", mp_rep)
                    if n >= 2:
                        wg("scalar", git0 + (n - 2) * 4 + 3 + 1)
                    ev_gate[n] = s_mark(lambda E, n=n: E.activation(
                        out=c_sb[:, (n % 2) * C:(n % 2) * C + C],
                        in_=psB[:, 0:1024], func=AF.Copy))
                    for dti in range(4):
                        git = git0 + n * 4 + dti
                        sl = git % 2
                        if git >= 2:
                            wv("scalar", da_gate[git - 2])
                        ma_da = s_mark(lambda E, dti=dti, n=n, sl=sl: E.activation(
                            out=da_sl[:, sl * C:(sl + 1) * C],
                            in_=dt_f[:, dti * C:(dti + 1) * C],
                            func=AF.Exp,
                            scale=aneg[:, dti * 16 + n:dti * 16 + n + 1]))
                        wa("vector", ma_da)
                        wp("vector", mp_rep)
                        if git >= 2:
                            wg("vector", git - 1)

                        def scan_body(E, dti=dti, n=n, sl=sl,
                                      prev=pend_copy[0]):
                            E.tensor_tensor(out=dbx_sl[:, sl * C:(sl + 1) * C],
                                            in0=u_f[:, dti * C:(dti + 1) * C],
                                            in1=psA[:, 0:1024], op=ALU.mult)
                            if prev is not None:
                                pd, pn, psl = prev
                                E.tensor_copy(
                                    hstate[:, pd * 16 + pn:pd * 16 + pn + 1],
                                    h_sl[:, (psl + 1) * C - 1:(psl + 1) * C])
                            return E.tensor_tensor_scan(
                                out=h_sl[:, sl * C:(sl + 1) * C],
                                data0=da_sl[:, sl * C:(sl + 1) * C],
                                data1=dbx_sl[:, sl * C:(sl + 1) * C],
                                initial=hstate[:, dti * 16 + n:dti * 16 + n + 1],
                                op0=ALU.mult, op1=ALU.add)
                        pend_copy[0] = (dti, n, sl)
                        mv_scan = v_mark(scan_body)
                        da_gate[git] = mv_scan
                        if dti < 3:
                            wv("gpsimd", mv_scan)
                            wa("gpsimd", ev_gate[n])
                            tp.op("gpsimd", lambda E, dti=dti, sl=sl, n=n:
                                  E.tensor_tensor(
                                      out=hc_sl[:, sl * C:(sl + 1) * C],
                                      in0=h_sl[:, sl * C:(sl + 1) * C],
                                      in1=c_sb[:, (n % 2) * C:(n % 2) * C + C],
                                      op=ALU.mult))
                            g_mark(lambda E, dti=dti, sl=sl: E.tensor_tensor(
                                out=ya_f[:, dti * C:(dti + 1) * C],
                                in0=ya_f[:, dti * C:(dti + 1) * C],
                                in1=hc_sl[:, sl * C:(sl + 1) * C], op=ALU.add))
                        else:
                            tp.op("vector", lambda E, dti=dti, sl=sl:
                                  E.tensor_tensor(
                                      out=hc_sl[:, sl * C:(sl + 1) * C],
                                      in0=h_sl[:, sl * C:(sl + 1) * C],
                                      in1=psB[:, 0:1024], op=ALU.mult))
                            tp.op("vector", lambda E, dti=dti, sl=sl:
                                  E.tensor_tensor(
                                      out=ya_f[:, dti * C:(dti + 1) * C],
                                      in0=ya_f[:, dti * C:(dti + 1) * C],
                                      in1=hc_sl[:, sl * C:(sl + 1) * C], op=ALU.add))
                            tp.mark("vector", "gp")
                def flush_copy(E, prev=pend_copy[0]):
                    pd, pn, psl = prev
                    return E.tensor_copy(
                        hstate[:, pd * 16 + pn:pd * 16 + pn + 1],
                        h_sl[:, (psl + 1) * C - 1:(psl + 1) * C])
                v_mark(flush_copy)
                pend_copy[0] = None
                wg("tensor", git0 + NST * 4)
                for dti in range(4):
                    for g in range(2):
                        for j in range(4):
                            tt = g * 4 + j
                            p_op(lambda E, dti=dti, tt=tt, j=j: E.transpose(
                                out=psT[:, j * 128:(j + 1) * 128],
                                in_=ya_f[:, dti * C + tt * 128:dti * C + tt * 128 + 128],
                                identity=ident_f[:]))
                        mp = p_mark()
                        wp("scalar", mp)

                        def evac_y(E, dti=dti, g=g):
                            for j in range(4):
                                tt = g * 4 + j
                                last = E.activation(
                                    out=scr[:, tt * DQ + dti * 128:tt * DQ + dti * 128 + 128],
                                    in_=psT[:, j * 128:(j + 1) * 128], func=AF.Copy)
                            return last
                        ma = s_mark(evac_y)
                        wa("tensor", ma)
                ma_yT = tp.cur("ac")
                wa("vector", ma_yT)
                for tt in range(NTT):
                    v_op(lambda E, k=k, tt=tt: E.reduce_sum(
                        st1[:, k * 24 + tt:k * 24 + tt + 1],
                        scr[:, tt * DQ:(tt + 1) * DQ], axis=AX.X))
                    v_op(lambda E, k=k, tt=tt: E.tensor_tensor(
                        out=sm_scr[:, 0:512],
                        in0=scr[:, tt * DQ:(tt + 1) * DQ],
                        in1=xT_t[:, tt * DQ:(tt + 1) * DQ], op=ALU.mult))
                    v_op(lambda E, k=k, tt=tt: E.reduce_sum(
                        st1[:, k * 24 + 16 + tt:k * 24 + 17 + tt],
                        sm_scr[:, 0:512], axis=AX.X))
                mv_st = v_mark()
                wv("scalar", mv_st)
                for tt in range(NTT):
                    s_op(lambda E, k=k, tt=tt: E.activation(
                        out=sm_scr[:, 0:512],
                        in_=scr[:, tt * DQ:(tt + 1) * DQ], func=AF.Square,
                        accum_out=st1[:, k * 24 + 8 + tt:k * 24 + 9 + tt]))
                ma_st = s_mark()
                wa("gpsimd", ma_st)
                wv("gpsimd", mv_st)
                gstore(yb_chunks[k], scr_v)
            wv("gpsimd", tp.cur("dv"))
            wa("gpsimd", tp.cur("ac"))
            gstore(cc1_in.ap(), st1[:])
            wst("gpsimd")
            cc_s1 = collective("AllReduce", ALU.add, cc1_in, cc1_out)
            if DEBUG_STOP == "P2":
                tp.wait("gpsimd", "cc", cc_s1)
                maybe_stop("P2", [ybuf])
            wc("sync", cc_s1)
            sload(st1[:], cc1_out.ap())

            for k in range(NCH):
                tp.wait("sync", "dv", tp.cur("dv"))
                tp.wait("sync", "ac", tp.cur("ac"))
                tp.wait("sync", "st", tp.cur("st"))
                sload(xT_v, xT_chunks[k])
                sload(scr_v, yb_chunks[k])
                wl("vector")

                mu1 = lncol[:, 0:8]
                var1 = lncol[:, 8:16]
                rr = lncol[:, 16:24]
                m1 = lncol[:, 24:32]
                t1 = lncol[:, 32:40]
                t2 = lncol[:, 40:48]
                mv = vseq(
                    lambda E, k=k: E.tensor_scalar_mul(
                        mu1, st1[:, k * 24:k * 24 + 8], 1.0 / D),
                    lambda E, k=k: E.tensor_scalar_mul(
                        var1, st1[:, k * 24 + 8:k * 24 + 16], 1.0 / D),
                    lambda E: E.tensor_tensor(out=lncol[:, 56:64], in0=mu1,
                                              in1=mu1, op=ALU.mult),
                    lambda E: E.tensor_tensor(out=var1, in0=var1,
                                              in1=lncol[:, 56:64], op=ALU.subtract),
                    lambda E: E.tensor_copy(rr, var1),
                    lambda E: E.tensor_scalar_add(rr, rr, EPS),
                )
                wv("scalar", mv)
                ma = s_mark(lambda E: E.activation(out=lncol[:, 16:24],
                                                   in_=lncol[:, 16:24], func=AF.Sqrt))
                wa("vector", ma)
                mv = vseq(
                    lambda E: E.reciprocal(rr, rr),
                    lambda E: E.tensor_tensor(out=m1, in0=mu1, in1=rr, op=ALU.mult),
                    lambda E: E.tensor_scalar_mul(m1, m1, -1.0),
                    lambda E, k=k: E.tensor_tensor(
                        out=t1, in0=mu1, in1=st1[:, k * 24:k * 24 + 8], op=ALU.mult),
                    lambda E, k=k: E.tensor_tensor(
                        out=t1, in0=st1[:, k * 24 + 16:k * 24 + 24], in1=t1,
                        op=ALU.subtract),
                    lambda E: E.tensor_tensor(out=t1, in0=t1, in1=rr, op=ALU.mult),
                    lambda E: E.tensor_scalar_mul(t1, t1, 2.0),
                    lambda E: E.tensor_tensor(out=t2, in0=var1, in1=rr, op=ALU.mult),
                    lambda E: E.tensor_tensor(out=t2, in0=t2, in1=rr, op=ALU.mult),
                    lambda E: E.tensor_scalar_mul(t2, t2, float(D)),
                    lambda E: E.tensor_tensor(out=t1, in0=t1, in1=t2, op=ALU.add),
                    lambda E, k=k: E.tensor_tensor(
                        out=t1, in0=st0[:, k * 16 + 8:k * 16 + 16], in1=t1,
                        op=ALU.add),
                    lambda E: E.tensor_scalar_mul(t1, t1, 1.0 / D),
                    lambda E, k=k: E.tensor_tensor(
                        out=t2, in0=mu0[:, k * 8:(k + 1) * 8],
                        in1=mu0[:, k * 8:(k + 1) * 8], op=ALU.mult),
                    lambda E: E.tensor_tensor(out=t1, in0=t1, in1=t2,
                                              op=ALU.subtract),
                    lambda E: E.tensor_scalar_add(t1, t1, EPS),
                )
                wv("scalar", mv)
                ma = s_mark(lambda E: E.activation(out=lncol[:, 32:40],
                                                   in_=lncol[:, 32:40], func=AF.Sqrt))
                wa("vector", ma)
                vseq(
                    lambda E: E.reciprocal(t1, t1),
                    lambda E, k=k: E.tensor_tensor(
                        out=t2, in0=mu0[:, k * 8:(k + 1) * 8], in1=t1, op=ALU.mult),
                    lambda E: E.tensor_scalar_mul(t2, t2, -1.0),
                )
                for tt in range(NTT):
                    def x1_body(E, k=k, tt=tt):
                        seg_y = scr[:, tt * DQ:(tt + 1) * DQ]
                        seg_x = xT_t[:, tt * DQ:(tt + 1) * DQ]
                        E.tensor_scalar(out=seg_y, in0=seg_y,
                                        scalar1=lncol[:, 16 + tt:17 + tt],
                                        scalar2=lncol[:, 24 + tt:25 + tt],
                                        op0=ALU.mult, op1=ALU.add)
                        E.tensor_tensor(out=seg_y, in0=seg_y, in1=seg_x, op=ALU.add)
                        return E.tensor_scalar(
                            out=x1n_b[:, tt * DQ:(tt + 1) * DQ], in0=seg_y,
                            scalar1=lncol[:, 32 + tt:33 + tt],
                            scalar2=lncol[:, 40 + tt:41 + tt],
                            op0=ALU.mult, op1=ALU.add)
                    v_mark(x1_body)
                mv_x1 = tp.cur("dv")
                if DEBUG_STOP == "P3s" and k == 0:
                    wv("gpsimd", mv_x1)
                    gstore(out_flat[0:128 * 64].rearrange("(a b) -> a b", a=128),
                           lncol[:])
                    gstore(out_flat[128 * 64:128 * 64 + 128 * 4096].rearrange(
                        "(a b) -> a b", a=128), scr[:])
                    gstore(out_flat[128 * 64 + 128 * 4096:
                                    128 * 64 + 128 * 4096 + 128 * 24 * NCH].rearrange(
                        "(a b) -> a b", a=128), st1[:])
                    tp.wait("gpsimd", "st", tp.cur("st"))
                    raise _StopBuild
                wv("gpsimd", mv_x1)
                gstore(x1_chunks[k], scr_v)
                wv("tensor", mv_x1)
                for dti in range(4):
                    for tt in range(NTT):
                        p_op(lambda E, dti=dti, tt=tt: E.transpose(
                            out=psX[:, tt * 128:(tt + 1) * 128],
                            in_=x1n_b[:, tt * DQ + dti * 128:tt * DQ + dti * 128 + 128],
                            identity=ident_b[:]))
                    mp = p_mark()
                    wp("scalar", mp)
                    ma = s_mark(lambda E, dti=dti: E.activation(
                        out=up_b[:, dti * C:(dti + 1) * C], in_=psX[:, 0:1024],
                        func=AF.Copy))
                    wa("tensor", ma)
                    wa("gpsimd", ma)
                    gstore(agx_in.ap()[dti * 128:(dti + 1) * 128, k * C:(k + 1) * C],
                           up_b[:, dti * C:(dti + 1) * C])
            wst("gpsimd")
            cc_agx = collective("AllGather", ALU.bypass, agx_in, agx_out)
            if DEBUG_STOP == "P3":
                tp.wait("gpsimd", "cc", cc_agx)
                maybe_stop("P3", [x1buf])

            mm_hist = []
            tp.wait("sync", "cc", cc_agx)
            for k in range(NCH):
                for nb in range(2):
                    for kt in range(16):
                        sl = kt % 2
                        if len(mm_hist) >= 2:
                            wp("sync", mm_hist[-2])
                        lv = sload(kt_a[:, sl * C:sl * C + 512],
                                   agx_out.ap()[kt * 128:(kt + 1) * 128,
                                                k * C + nb * 512:k * C + nb * 512 + 512])

                        def mm_up(E, kt=kt, sl=sl):
                            for mt in range(4):
                                ps = (psA, psA, psB, psB)[mt]
                                last = E.matmul(
                                    out=ps[:, (mt % 2) * 512:(mt % 2) * 512 + 512],
                                    lhsT=wup_bf[:, kt * DQ + mt * 128:kt * DQ + mt * 128 + 128],
                                    rhs=kt_a[:, sl * C:sl * C + 512],
                                    start=(kt == 0), stop=(kt == 15))
                            return last
                        tp.wait("tensor", "ld", lv)
                        mm_hist.append(p_mark(mm_up))
                    wp("scalar", mm_hist[-1])

                    def evac_up(E, nb=nb):
                        for mt in range(4):
                            ps = (psA, psA, psB, psB)[mt]
                            last = E.activation(
                                out=up_b[:, mt * C + nb * 512:mt * C + nb * 512 + 512],
                                in_=ps[:, (mt % 2) * 512:(mt % 2) * 512 + 512],
                                func=AF.Relu)
                        return last
                    ma = s_mark(evac_up)
                    wa("tensor", ma)
                    wa("gpsimd", ma)
                    for mt in range(4):
                        gstore(agu_in.ap()[mt * 128:(mt + 1) * 128,
                                           k * C + nb * 512:k * C + nb * 512 + 512],
                               up_b[:, mt * C + nb * 512:mt * C + nb * 512 + 512])
            wst("gpsimd")
            cc_agu = collective("AllGather", ALU.bypass, agu_in, agu_out)
            if DEBUG_STOP == "P4":
                tp.wait("gpsimd", "cc", cc_agu)
                maybe_stop("P4", [agu_in])

            p4_last_mm = mm_hist[-1]
            mm_hist = []
            tp.wait("sync", "cc", cc_agu)
            wp("sync", p4_last_mm)
            for k in range(NCH):
                for nb in range(2):
                    for kt in range(16):
                        sl = kt % 2
                        if len(mm_hist) >= 2:
                            wp("sync", mm_hist[-2])
                        sload(kt_a[:, sl * C:sl * C + 512],
                              agu_out.ap()[kt * 128:(kt + 1) * 128,
                                           k * C + nb * 512:k * C + nb * 512 + 512])
                        lv2 = sload(kt_b[:, sl * C:sl * C + 512],
                                    agx_out.ap()[kt * 128:(kt + 1) * 128,
                                                 k * C + nb * 512:k * C + nb * 512 + 512])

                        def mm_f(E, kt=kt, sl=sl):
                            for mt in range(4):
                                ps = (psA, psA, psB, psB)[mt]
                                E.matmul(
                                    out=ps[:, (mt % 2) * 512:(mt % 2) * 512 + 512],
                                    lhsT=wdn_bf[:, kt * DQ + mt * 128:kt * DQ + mt * 128 + 128],
                                    rhs=kt_a[:, sl * C:sl * C + 512],
                                    start=(kt == 0), stop=False)
                                last = E.matmul(
                                    out=ps[:, (mt % 2) * 512:(mt % 2) * 512 + 512],
                                    lhsT=wex_bf[:, kt * DQ + mt * 128:kt * DQ + mt * 128 + 128],
                                    rhs=kt_b[:, sl * C:sl * C + 512],
                                    start=False, stop=(kt == 15))
                            return last
                        tp.wait("tensor", "ld", lv2)
                        mm_hist.append(p_mark(mm_f))
                    wp("vector", mm_hist[-1])

                    def evac_f(E, nb=nb):
                        for mt in range(4):
                            ps = (psA, psA, psB, psB)[mt]
                            last = E.tensor_copy(
                                u_f[:, mt * C + nb * 512:mt * C + nb * 512 + 512],
                                ps[:, (mt % 2) * 512:(mt % 2) * 512 + 512])
                        return last
                    mv = v_mark(evac_f)
                    wv("tensor", mv)
                mv_f = tp.cur("dv")
                wv("tensor", mv_f)
                wst("tensor")
                for dti in range(4):
                    for g in range(2):
                        for j in range(4):
                            tt = g * 4 + j
                            p_op(lambda E, dti=dti, tt=tt, j=j: E.transpose(
                                out=psT[:, j * 128:(j + 1) * 128],
                                in_=u_f[:, dti * C + tt * 128:dti * C + tt * 128 + 128],
                                identity=ident_f[:]))
                        mp = p_mark()
                        wp("scalar", mp)

                        def evac_fT(E, dti=dti, g=g):
                            for j in range(4):
                                tt = g * 4 + j
                                last = E.activation(
                                    out=scr[:, tt * DQ + dti * 128:tt * DQ + dti * 128 + 128],
                                    in_=psT[:, j * 128:(j + 1) * 128], func=AF.Copy)
                            return last
                        ma = s_mark(evac_fT)
                        wa("tensor", ma)
                ma_fT = tp.cur("ac")
                wa("vector", ma_fT)
                for tt in range(NTT):
                    v_op(lambda E, k=k, tt=tt: E.reduce_sum(
                        st3[:, k * 16 + tt:k * 16 + tt + 1],
                        scr[:, tt * DQ:(tt + 1) * DQ], axis=AX.X))
                mv_s3 = v_mark()
                wv("scalar", mv_s3)
                for tt in range(NTT):
                    s_op(lambda E, k=k, tt=tt: E.activation(
                        out=sm_scr[:, 0:512],
                        in_=scr[:, tt * DQ:(tt + 1) * DQ], func=AF.Square,
                        accum_out=st3[:, k * 16 + 8 + tt:k * 16 + 9 + tt]))
                ma_s3 = s_mark()
                wa("gpsimd", ma_s3)
                wv("gpsimd", mv_s3)
                gstore(fb_chunks[k], scr_v)
            wv("gpsimd", tp.cur("dv"))
            wa("gpsimd", tp.cur("ac"))
            gstore(cc3_in.ap(), st3[:])
            wst("gpsimd")
            cc_s3 = collective("AllReduce", ALU.add, cc3_in, cc3_out)
            if DEBUG_STOP == "P5a":
                tp.wait("gpsimd", "cc", cc_s3)
                maybe_stop("P5a", [fbuf])
            wc("sync", cc_s3)
            sload(st3[:], cc3_out.ap())

            for k in range(NCH):
                tp.wait("sync", "dv", tp.cur("dv"))
                tp.wait("sync", "st", tp.cur("st"))
                sload(xT_v, x1_chunks[k])
                sload(scr_v, fb_chunks[k])
                wl("vector")

                mv = vseq(
                    lambda E, k=k: E.tensor_scalar_mul(
                        lncol[:, 0:8], st3[:, k * 16:k * 16 + 8], 1.0 / D),
                    lambda E, k=k: E.tensor_scalar_mul(
                        lncol[:, 8:16], st3[:, k * 16 + 8:k * 16 + 16], 1.0 / D),
                    lambda E: E.tensor_tensor(out=lncol[:, 56:64], in0=lncol[:, 0:8],
                                              in1=lncol[:, 0:8], op=ALU.mult),
                    lambda E: E.tensor_tensor(out=lncol[:, 8:16], in0=lncol[:, 8:16],
                                              in1=lncol[:, 56:64], op=ALU.subtract),
                    lambda E: E.tensor_copy(lncol[:, 16:24], lncol[:, 8:16]),
                    lambda E: E.tensor_scalar_add(lncol[:, 16:24], lncol[:, 16:24],
                                                  EPS),
                )
                wv("scalar", mv)
                ma = s_mark(lambda E: E.activation(out=lncol[:, 16:24],
                                                   in_=lncol[:, 16:24], func=AF.Sqrt))
                wa("vector", ma)
                vseq(
                    lambda E: E.reciprocal(lncol[:, 16:24], lncol[:, 16:24]),
                    lambda E: E.tensor_tensor(out=lncol[:, 24:32], in0=lncol[:, 0:8],
                                              in1=lncol[:, 16:24], op=ALU.mult),
                    lambda E: E.tensor_scalar_mul(lncol[:, 24:32], lncol[:, 24:32],
                                                  -1.0),
                )

                def outb(E, k=k):
                    rr = lncol[:, 16:24]
                    m3 = lncol[:, 24:32]
                    for tt in range(NTT):
                        seg_f = scr[:, tt * DQ:(tt + 1) * DQ]
                        E.tensor_scalar(out=seg_f, in0=seg_f,
                                        scalar1=rr[:, tt:tt + 1],
                                        scalar2=m3[:, tt:tt + 1],
                                        op0=ALU.mult, op1=ALU.add)
                        last = E.tensor_tensor(out=seg_f, in0=seg_f,
                                        in1=xT_t[:, tt * DQ:(tt + 1) * DQ], op=ALU.add)
                    return last
                mv = v_mark(outb)
                wv("gpsimd", mv)
                gstore(out_hbm.rearrange("(k a p) d -> k p a d", p=128, a=NTT)[k],
                       scr_v)
            final_st = tp.cur("st")
            tp.wait("gpsimd", "st", final_st)


        try:
            emit()
        except _StopBuild:
            pass

        @block.sync
        def _(E):
            tp.replay("sync", E, SEMS)

        @block.gpsimd
        def _(E):
            tp.replay("gpsimd", E, SEMS)

        @block.vector
        def _(E):
            tp.replay("vector", E, SEMS)

        @block.scalar
        def _(E):
            tp.replay("scalar", E, SEMS)

        @block.tensor
        def _(E):
            tp.replay("tensor", E, SEMS)

    return nc


def _per_core_inputs(inputs, core):
    g, q = core // 4, core % 4
    dq = slice(DQ * q, DQ * (q + 1))
    x = np.asarray(inputs["L0_attn_x"])
    W_x = np.asarray(inputs["L0_attn_W_x"])
    W_dt = np.asarray(inputs["L0_attn_W_dt"])
    b_dt = np.asarray(inputs["L0_attn_b_dt"])
    A_log = np.asarray(inputs["L0_attn_A_log"])
    D_param = np.asarray(inputs["L0_attn_D_param"])
    W_up = np.asarray(inputs["L0_ffn_W_up"])
    W_down = np.asarray(inputs["L0_ffn_W_down"])
    W_expert = np.asarray(inputs["L0_ffn_W_expert"])

    sel = np.zeros((32, 32 * 128), np.float32)
    for n in range(32):
        sel[n, n * 128:(n + 1) * 128] = 1.0

    return {
        "xT_in": np.ascontiguousarray(x[g][:, dq]),
        "wx_in": np.ascontiguousarray(W_x.T[dq, :]),
        "wdt_in": np.ascontiguousarray(W_dt[dq, :].T),
        "bdt_in": np.ascontiguousarray(b_dt[dq].reshape(4, 128).T),
        "alog_in": np.ascontiguousarray(
            A_log[dq].reshape(4, 128, NST).transpose(1, 0, 2).reshape(128, 64)),
        "dpar_in": np.ascontiguousarray(D_param[dq].reshape(4, 128).T),
        "wup_in": np.ascontiguousarray(W_up[dq, :].T),
        "wdn_in": np.ascontiguousarray(W_down[dq, :].T),
        "wex_in": np.ascontiguousarray(W_expert[:, dq]),
        "ident_in": np.eye(128, dtype=np.float32),
        "sel32_in": sel,
    }


def kernel(**inputs):
    nc = build()
    in_maps = [_per_core_inputs(inputs, c) for c in range(8)]
    res = run_bass_kernel_spmd(nc, in_maps, core_ids=list(range(8)))
    out = np.empty((B, T, D), np.float32)
    for c in range(8):
        g, q = c // 4, c % 4
        out[g][:, DQ * q:DQ * (q + 1)] = res.results[c]["outT"]
    return out


if __name__ == "__main__":
    nc = build()
    print("build ok; instrs:",
          sum(1 for _ in nc.all_instructions()) if hasattr(nc, "all_instructions") else "?")


# version 07
# speedup vs baseline: 1.2217x; 1.2217x over previous
import numpy as np
import concourse.bass as bass
import concourse.mybir as mybir
from concourse.bass_utils import run_bass_kernel_spmd

F32 = mybir.dt.float32
BF16 = mybir.dt.bfloat16
ALU = mybir.AluOpType
AF = mybir.ActivationFunctionType
AX = mybir.AxisListType

B = 2
T = 8192
D = 2048
DQ = 512
NST = 16
RK = 128
C = 1024
NCH = T // C
NTT = C // 128
EPS = 1e-5
RG = [[0, 1, 2, 3], [4, 5, 6, 7]]

ENGINES = ("sync", "gpsimd", "vector", "scalar", "tensor")

YA_ON_GP = True
DEBUG_STOP = None


class _StopBuild(Exception):
    pass


class Tape:

    def __init__(self):
        self.q = {e: [] for e in ENGINES}
        self.cnt = {}

    def op(self, eng, fn):
        self.q[eng].append([fn, None, 0])

    def mark(self, eng, sem, amt=1):
        ent = self.q[eng][-1]
        assert ent[1] is None, "op already carries an inc"
        self.cnt[sem] = self.cnt.get(sem, 0) + amt
        ent[1], ent[2] = sem, amt
        return self.cnt[sem]

    def inc(self, eng, sem, fn, amt):
        self.op(eng, fn)
        return self.mark(eng, sem, amt)

    def wait(self, eng, sem, val):
        if val is None or val <= 0:
            return
        self.q[eng].append([("wait", sem, val), None, 0])

    def cur(self, sem):
        return self.cnt.get(sem, 0)

    def replay(self, eng, E, sems):
        for fn, sem, amt in self.q[eng]:
            if isinstance(fn, tuple):
                E.wait_ge(sems[fn[1]], fn[2])
                continue
            inst = fn(E)
            if sem is not None:
                assert inst is not None, f"{eng} op with inc returned no inst"
                inst.then_inc(sems[sem], amt)


def build():
    nc = bass.Bass("TRN2", target_bir_lowering=False)

    xT_in = nc.dram_tensor("xT_in", [T, DQ], F32, kind="ExternalInput").ap()
    wx_in = nc.dram_tensor("wx_in", [DQ, 160], F32, kind="ExternalInput").ap()
    wdt_in = nc.dram_tensor("wdt_in", [RK, DQ], F32, kind="ExternalInput").ap()
    bdt_in = nc.dram_tensor("bdt_in", [128, 4], F32, kind="ExternalInput").ap()
    alog_in = nc.dram_tensor("alog_in", [128, 64], F32, kind="ExternalInput").ap()
    dpar_in = nc.dram_tensor("dpar_in", [128, 4], F32, kind="ExternalInput").ap()
    wup_in = nc.dram_tensor("wup_in", [D, DQ], F32, kind="ExternalInput").ap()
    wdn_in = nc.dram_tensor("wdn_in", [D, DQ], F32, kind="ExternalInput").ap()
    wex_in = nc.dram_tensor("wex_in", [D, DQ], F32, kind="ExternalInput").ap()
    ident_in = nc.dram_tensor("ident_in", [128, 128], F32, kind="ExternalInput").ap()
    sel32_in = nc.dram_tensor("sel32_in", [32, 32 * 128], F32, kind="ExternalInput").ap()
    out_hbm = nc.dram_tensor("outT", [T, DQ], F32, kind="ExternalOutput").ap()

    cc0_in = nc.dram_tensor("cc0_in", [128, 16 * NCH], F32, kind="Internal")
    cc0_out = nc.dram_tensor("cc0_out", [128, 16 * NCH], F32, kind="Internal")
    ccxz_in = nc.dram_tensor("ccxz_in", [160, T], F32, kind="Internal")
    ccxz_out = nc.dram_tensor("ccxz_out", [160, T], F32, kind="Internal")
    cc1_in = nc.dram_tensor("cc1_in", [128, 24 * NCH], F32, kind="Internal")
    cc1_out = nc.dram_tensor("cc1_out", [128, 24 * NCH], F32, kind="Internal")
    cc3_in = nc.dram_tensor("cc3_in", [128, 16 * NCH], F32, kind="Internal")
    cc3_out = nc.dram_tensor("cc3_out", [128, 16 * NCH], F32, kind="Internal")
    agx_in = nc.dram_tensor("agx_in", [DQ, T], BF16, kind="Internal")
    agx_out = nc.dram_tensor("agx_out", [D, T], BF16, kind="Internal")
    agu_in = nc.dram_tensor("agu_in", [DQ, T], BF16, kind="Internal")
    agu_out = nc.dram_tensor("agu_out", [D, T], BF16, kind="Internal")
    ybuf = nc.dram_tensor("ybuf", [T, DQ], F32, kind="Internal")
    x1buf = nc.dram_tensor("x1buf", [T, DQ], F32, kind="Internal")
    fbuf = nc.dram_tensor("fbuf", [T, DQ], F32, kind="Internal")

    xT_tiles = xT_in.rearrange("(n p) d -> n p d", p=128)
    yb_tiles = ybuf.ap().rearrange("(n p) d -> n p d", p=128)
    x1_tiles = x1buf.ap().rearrange("(n p) d -> n p d", p=128)
    fb_tiles = fbuf.ap().rearrange("(n p) d -> n p d", p=128)
    out_tiles = out_hbm.rearrange("(n p) d -> n p d", p=128)

    tp = Tape()

    from contextlib import ExitStack
    with ExitStack() as ctx:
        wup_bf = ctx.enter_context(nc.sbuf_tensor("wup_bf", [128, 16 * DQ], BF16))
        wdn_bf = ctx.enter_context(nc.sbuf_tensor("wdn_bf", [128, 16 * DQ], BF16))
        wex_bf = ctx.enter_context(nc.sbuf_tensor("wex_bf", [128, 16 * DQ], BF16))
        wx_bf = ctx.enter_context(nc.sbuf_tensor("wx_bf", [128, 4 * 160], BF16))
        wdt_bf = ctx.enter_context(nc.sbuf_tensor("wdt_bf", [128, DQ], BF16))
        bdt_t = ctx.enter_context(nc.sbuf_tensor("bdt_t", [128, 4], F32))
        aneg = ctx.enter_context(nc.sbuf_tensor("aneg", [128, 64], F32))
        dpar_t = ctx.enter_context(nc.sbuf_tensor("dpar_t", [128, 4], F32))
        ident_f = ctx.enter_context(nc.sbuf_tensor("ident_f", [128, 128], F32))
        ident_b = ctx.enter_context(nc.sbuf_tensor("ident_b", [128, 128], BF16))
        sel32 = ctx.enter_context(nc.sbuf_tensor("sel32", [32, 32 * 128], BF16))
        st0 = ctx.enter_context(nc.sbuf_tensor("st0", [128, 16 * NCH], F32))
        mu0 = ctx.enter_context(nc.sbuf_tensor("mu0", [128, 8 * NCH], F32))
        r0c = ctx.enter_context(nc.sbuf_tensor("r0c", [128, 8 * NCH], F32))
        st1 = ctx.enter_context(nc.sbuf_tensor("st1", [128, 24 * NCH], F32))
        st3 = ctx.enter_context(nc.sbuf_tensor("st3", [128, 16 * NCH], F32))
        lncol = ctx.enter_context(nc.sbuf_tensor("lncol", [128, 64], F32))
        xT_t = ctx.enter_context(nc.sbuf_tensor("xT_t", [128, NTT * DQ], F32))
        scr = ctx.enter_context(nc.sbuf_tensor("scr", [128, NTT * DQ], F32))
        pool = ctx.enter_context(nc.sbuf_tensor("pool", [128, 4 * C], F32))
        dt_f = ctx.enter_context(nc.sbuf_tensor("dt_f", [128, 4 * C], F32))
        u_f = ctx.enter_context(nc.sbuf_tensor("u_f", [128, 4 * C], F32))
        ya_f = ctx.enter_context(nc.sbuf_tensor("ya_f", [128, 4 * C], F32))
        da_sl = ctx.enter_context(nc.sbuf_tensor("da_sl", [128, 2 * C], F32))
        dbx_sl = ctx.enter_context(nc.sbuf_tensor("dbx_sl", [128, 2 * C], F32))
        sm_scr = ctx.enter_context(nc.sbuf_tensor("sm_scr", [128, 512], F32))
        dtraw_b = ctx.enter_context(nc.sbuf_tensor("dtraw_b", [128, C], BF16))
        bc_t = ctx.enter_context(nc.sbuf_tensor("bc_t", [32, C], F32))
        bc_h = ctx.enter_context(nc.sbuf_tensor("bc_h", [32, C], BF16))
        bc_l = ctx.enter_context(nc.sbuf_tensor("bc_l", [32, C], BF16))
        bc_l2 = ctx.enter_context(nc.sbuf_tensor("bc_l2", [32, C], BF16))
        xz32 = ctx.enter_context(nc.sbuf_tensor("xz32", [32, C], F32))
        x1n_b = ctx.enter_context(nc.sbuf_tensor("x1n_b", [128, 4 * C], BF16))
        up_b = ctx.enter_context(nc.sbuf_tensor("up_b", [128, 4 * C], BF16))
        psA = ctx.enter_context(nc.psum_tensor("psA", [128, 1024], F32))
        psB = ctx.enter_context(nc.psum_tensor("psB", [128, 1024], F32))
        psM = ctx.enter_context(nc.psum_tensor("psM", [128, 512], F32))
        psN = ctx.enter_context(nc.psum_tensor("psN", [128, 512], F32))
        psT = ctx.enter_context(nc.psum_tensor("psT", [128, 512], F32))
        psX = ctx.enter_context(nc.psum_tensor("psX", [128, 1024], BF16))
        SEM_ld = ctx.enter_context(nc.semaphore("ld"))
        SEM_st = ctx.enter_context(nc.semaphore("st"))
        SEM_cc = ctx.enter_context(nc.semaphore("cc"))
        SEM_pe = ctx.enter_context(nc.semaphore("pe"))
        SEM_dv = ctx.enter_context(nc.semaphore("dv"))
        SEM_ac = ctx.enter_context(nc.semaphore("ac"))
        SEM_gp = ctx.enter_context(nc.semaphore("gp"))
        SEM_gl = ctx.enter_context(nc.semaphore("gl"))
        block = ctx.enter_context(nc.Block())
        SEMS = {"ld": SEM_ld, "st": SEM_st, "cc": SEM_cc, "pe": SEM_pe,
                "dv": SEM_dv, "ac": SEM_ac, "gp": SEM_gp, "gl": SEM_gl}

        hstate = lncol
        out_flat = out_hbm.rearrange("a b -> (a b)")
        xT_chunks = xT_in.rearrange("(k a p) d -> k p a d", p=128, a=NTT)
        yb_chunks = ybuf.ap().rearrange("(k a p) d -> k p a d", p=128, a=NTT)
        x1_chunks = x1buf.ap().rearrange("(k a p) d -> k p a d", p=128, a=NTT)
        fb_chunks = fbuf.ap().rearrange("(k a p) d -> k p a d", p=128, a=NTT)
        xT_v = xT_t[:].rearrange("p (a d) -> p a d", a=NTT)
        scr_v = scr[:].rearrange("p (a d) -> p a d", a=NTT)

        def maybe_stop(tag, bufs):
            if DEBUG_STOP != tag:
                return
            off = 0
            for t in bufs:
                n = int(np.prod(t.shape))
                gstore(out_flat[off:off + n], t.ap().rearrange("a b -> (a b)"))
                off += n
            tp.wait("gpsimd", "st", tp.cur("st"))
            raise _StopBuild

        xn_f = pool
        h_sl = pool[:, 0:2 * C]
        hc_sl = pool[:, 2 * C:4 * C]
        kt_a = x1n_b[:, 0:2 * C]
        kt_b = x1n_b[:, 2 * C:4 * C]

        def sload(dst, src):
            return tp.inc("sync", "ld",
                          lambda E, d=dst, s=src: E.dma_start(out=d, in_=s), 16)

        def gstore(dst, src):
            return tp.inc("gpsimd", "st",
                          lambda E, d=dst, s=src: E.dma_start(out=d, in_=s), 16)

        def collective(kind, op, cin, cout):
            return tp.inc("gpsimd", "cc",
                          lambda E, k=kind, o=op, a=cin, b=cout:
                          E.collective_compute(k, o, ins=[a.ap()], outs=[b.ap()],
                                               replica_groups=RG), 1)

        def v_op(fn):
            tp.op("vector", fn)

        def v_mark(fn=None):
            if fn is not None:
                tp.op("vector", fn)
            return tp.mark("vector", "dv")

        def vseq(*fns):
            mv = None
            for fn in fns:
                tp.op("vector", fn)
                mv = tp.mark("vector", "dv")
                tp.wait("vector", "dv", mv)
            return mv

        def s_op(fn):
            tp.op("scalar", fn)

        def s_mark(fn=None):
            if fn is not None:
                tp.op("scalar", fn)
            return tp.mark("scalar", "ac")

        def p_op(fn):
            tp.op("tensor", fn)

        def p_mark(fn=None):
            if fn is not None:
                tp.op("tensor", fn)
            return tp.mark("tensor", "pe")

        def g_mark(fn):
            tp.op("gpsimd", fn)
            return tp.mark("gpsimd", "gp")

        def wv(eng, val):
            tp.wait(eng, "dv", val)

        def wa(eng, val):
            tp.wait(eng, "ac", val)

        def wp(eng, val):
            tp.wait(eng, "pe", val)

        def wg(eng, val):
            tp.wait(eng, "gp", val)

        def wl(eng):
            tp.wait(eng, "ld", tp.cur("ld"))

        def wst(eng):
            tp.wait(eng, "st", tp.cur("st"))

        def wc(eng, val):
            tp.wait(eng, "cc", val)

        def emit():
            last_cast = 0
            for wi, (src, dstb) in enumerate(
                    [(wup_in, wup_bf), (wdn_in, wdn_bf), (wex_in, wex_bf)]):
                for g4 in range(4):
                    if last_cast:
                        tp.wait("sync", "dv", last_cast)
                    for j in range(4):
                        kt = g4 * 4 + j
                        sload(scr[:, j * DQ:(j + 1) * DQ],
                              src[kt * 128:(kt + 1) * 128, :])
                    wl("vector")
                    for j in range(4):
                        kt = g4 * 4 + j
                        v_op(lambda E, j=j, kt=kt, dstb=dstb: E.tensor_copy(
                            dstb[:, kt * DQ:(kt + 1) * DQ],
                            scr[:, j * DQ:(j + 1) * DQ]))
                    last_cast = v_mark()
            tp.wait("sync", "dv", last_cast)
            for j in range(4):
                sload(scr[:, j * 160:(j + 1) * 160], wx_in[j * 128:(j + 1) * 128, :])
            sload(scr[:, 640:1152], wdt_in)
            sload(bdt_t[:], bdt_in)
            sload(scr[:, 1152:1216], alog_in)
            sload(dpar_t[:], dpar_in)
            sload(ident_f[:], ident_in)
            sload(xT_t[0:32, 0:4096], sel32_in)
            wl("vector")
            for j in range(4):
                v_op(lambda E, j=j: E.tensor_copy(
                    wx_bf[:, j * 160:(j + 1) * 160], scr[:, j * 160:(j + 1) * 160]))
            v_op(lambda E: E.tensor_copy(wdt_bf[:], scr[:, 640:1152]))
            v_op(lambda E: E.tensor_copy(ident_b[:], ident_f[:]))
            v_op(lambda E: E.tensor_copy(sel32[:], xT_t[0:32, 0:4096]))
            mv0 = v_mark()
            wl("scalar")
            ma = s_mark(lambda E: E.activation(out=aneg[:], in_=scr[:, 1152:1216],
                                               func=AF.Exp))
            wa("vector", ma)
            v_op(lambda E: E.tensor_scalar_mul(aneg[:], aneg[:], -1.0))
            P0_dv = v_mark()

            for k in range(NCH):
                tp.wait("sync", "dv", tp.cur("dv"))
                tp.wait("sync", "ac", tp.cur("ac"))
                sload(xT_v, xT_chunks[k])
                wl("vector")
                for tt in range(NTT):
                    v_op(lambda E, k=k, tt=tt: E.reduce_sum(
                        st0[:, k * 16 + tt:k * 16 + tt + 1],
                        xT_t[:, tt * DQ:(tt + 1) * DQ], axis=AX.X))
                v_mark()
                wl("scalar")
                for tt in range(NTT):
                    s_op(lambda E, k=k, tt=tt: E.activation(
                        out=sm_scr[:, 0:512],
                        in_=xT_t[:, tt * DQ:(tt + 1) * DQ], func=AF.Square,
                        accum_out=st0[:, k * 16 + 8 + tt:k * 16 + 9 + tt]))
                s_mark()
            wv("gpsimd", tp.cur("dv"))
            wa("gpsimd", tp.cur("ac"))
            gstore(cc0_in.ap(), st0[:])
            wst("gpsimd")
            cc_v = collective("AllReduce", ALU.add, cc0_in, cc0_out)
            wc("sync", cc_v)
            sload(st0[:], cc0_out.ap())
            wl("vector")
            for k in range(NCH):
                mv = vseq(
                    lambda E, k=k: E.tensor_scalar_mul(
                        mu0[:, k * 8:(k + 1) * 8], st0[:, k * 16:k * 16 + 8],
                        1.0 / D),
                    lambda E, k=k: E.tensor_scalar_mul(
                        r0c[:, k * 8:(k + 1) * 8], st0[:, k * 16 + 8:k * 16 + 16],
                        1.0 / D),
                    lambda E, k=k: E.tensor_tensor(
                        out=lncol[:, 0:8], in0=mu0[:, k * 8:(k + 1) * 8],
                        in1=mu0[:, k * 8:(k + 1) * 8], op=ALU.mult),
                    lambda E, k=k: E.tensor_tensor(
                        out=r0c[:, k * 8:(k + 1) * 8],
                        in0=r0c[:, k * 8:(k + 1) * 8],
                        in1=lncol[:, 0:8], op=ALU.subtract),
                    lambda E, k=k: E.tensor_scalar_add(
                        r0c[:, k * 8:(k + 1) * 8], r0c[:, k * 8:(k + 1) * 8], EPS),
                )
            wv("scalar", mv)
            ma = s_mark(lambda E: E.activation(out=r0c[:], in_=r0c[:], func=AF.Sqrt))
            wa("vector", ma)
            mv = tp.inc("vector", "dv", lambda E: E.reciprocal(r0c[:], r0c[:]), 1)
            tp.wait("vector", "dv", mv)

            for k in range(NCH):
                tp.wait("sync", "dv", tp.cur("dv"))
                tp.wait("sync", "ac", tp.cur("ac"))
                sload(xT_v, xT_chunks[k])
                wl("vector")
                for tt in range(NTT):
                    v_op(lambda E, k=k, tt=tt: E.tensor_scalar(
                        out=x1n_b[:, tt * DQ:(tt + 1) * DQ],
                        in0=xT_t[:, tt * DQ:(tt + 1) * DQ],
                        scalar1=mu0[:, k * 8 + tt:k * 8 + tt + 1],
                        scalar2=r0c[:, k * 8 + tt:k * 8 + tt + 1],
                        op0=ALU.subtract, op1=ALU.mult))
                mv = v_mark()
                wv("tensor", mv)
                for dti in range(4):
                    for tt in range(NTT):
                        p_op(lambda E, dti=dti, tt=tt: E.transpose(
                            out=psX[:, tt * 128:(tt + 1) * 128],
                            in_=x1n_b[:, tt * DQ + dti * 128:tt * DQ + dti * 128 + 128],
                            identity=ident_b[:]))
                    mp = p_mark()
                    wp("scalar", mp)
                    ma = s_mark(lambda E, dti=dti: E.activation(
                        out=up_b[:, dti * C:(dti + 1) * C], in_=psX[:, 0:1024],
                        func=AF.Copy))
                    wa("tensor", ma)
                for nb in range(2):
                    def mm_xz(E, nb=nb):
                        for kt in range(4):
                            E.matmul(out=psM[:, 0:512],
                                     lhsT=wx_bf[:, kt * 160:kt * 160 + 128],
                                     rhs=up_b[:, kt * C + nb * 512:kt * C + nb * 512 + 512],
                                     start=(kt == 0), stop=(kt == 3))
                        for kt in range(4):
                            last = E.matmul(out=psN[0:32, 0:512],
                                     lhsT=wx_bf[:, kt * 160 + 128:kt * 160 + 160],
                                     rhs=up_b[:, kt * C + nb * 512:kt * C + nb * 512 + 512],
                                     start=(kt == 0), stop=(kt == 3))
                        return last
                    mp = p_mark(mm_xz)
                    wp("vector", mp)
                    v_op(lambda E, nb=nb: E.tensor_copy(
                        scr[:, nb * 512:(nb + 1) * 512], psM[:, 0:512]))
                    v_op(lambda E, nb=nb: E.tensor_copy(
                        xz32[:, nb * 512:(nb + 1) * 512], psN[0:32, 0:512]))
                    mv = v_mark()
                    wv("tensor", mv)
                    wv("gpsimd", mv)
                    gstore(ccxz_in.ap()[0:128, k * C + nb * 512:k * C + nb * 512 + 512],
                           scr[:, nb * 512:(nb + 1) * 512])
                gstore(ccxz_in.ap()[128:160, k * C:(k + 1) * C], xz32[:])
            wst("gpsimd")
            cc_xz = collective("AllReduce", ALU.add, ccxz_in, ccxz_out)
            if DEBUG_STOP == "P1b":
                tp.wait("gpsimd", "cc", cc_xz)
                maybe_stop("P1b", [ccxz_out, cc0_out])

            v_op(lambda E: E.memset(hstate[:], 0.0))
            v_mark()
            da_gate = {}
            for k in range(NCH):
                git0 = k * NST * 4
                tp.wait("sync", "dv", tp.cur("dv"))
                tp.wait("sync", "ac", tp.cur("ac"))
                tp.wait("sync", "gp", git0)
                if k == 0:
                    tp.wait("sync", "cc", cc_xz)
                sload(xT_v, xT_chunks[k])
                sload(dbx_sl[:, 0:C], ccxz_out.ap()[0:128, k * C:(k + 1) * C])
                sload(bc_t[:], ccxz_out.ap()[128:160, k * C:(k + 1) * C])
                wl("vector")
                if k == 0:
                    wc("vector", cc_xz)
                for tt in range(NTT):
                    v_op(lambda E, k=k, tt=tt: E.tensor_scalar(
                        out=scr[:, tt * DQ:(tt + 1) * DQ],
                        in0=xT_t[:, tt * DQ:(tt + 1) * DQ],
                        scalar1=mu0[:, k * 8 + tt:k * 8 + tt + 1],
                        scalar2=r0c[:, k * 8 + tt:k * 8 + tt + 1],
                        op0=ALU.subtract, op1=ALU.mult))
                v_op(lambda E: E.tensor_copy(bc_h[:], bc_t[:]))
                v_op(lambda E: E.tensor_copy(xz32[:], bc_h[:]))
                v_op(lambda E: E.tensor_tensor(out=xz32[:], in0=bc_t[:], in1=xz32[:],
                                               op=ALU.subtract))
                v_op(lambda E: E.tensor_copy(bc_l[:], xz32[:]))
                v_op(lambda E: E.tensor_copy(bc_t[:], bc_l[:]))
                v_op(lambda E: E.tensor_tensor(out=xz32[:], in0=xz32[:], in1=bc_t[:],
                                               op=ALU.subtract))
                v_op(lambda E: E.tensor_copy(bc_l2[:], xz32[:]))
                v_op(lambda E: E.tensor_copy(dtraw_b[:], dbx_sl[:, 0:C]))
                mv_pre = v_mark()
                wv("tensor", mv_pre)
                wg("scalar", git0)
                for dti in range(4):
                    for g in range(2):
                        for j in range(4):
                            tt = g * 4 + j
                            p_op(lambda E, dti=dti, tt=tt, j=j: E.transpose(
                                out=psT[:, j * 128:(j + 1) * 128],
                                in_=scr[:, tt * DQ + dti * 128:tt * DQ + dti * 128 + 128],
                                identity=ident_f[:]))
                        mp = p_mark()
                        wp("scalar", mp)
                        ma = s_mark(lambda E, dti=dti, g=g: E.activation(
                            out=xn_f[:, dti * C + g * 512:dti * C + g * 512 + 512],
                            in_=psT[:, 0:512], func=AF.Copy))
                        wa("tensor", ma)
                for dti in range(4):
                    for nb in range(2):
                        mp = p_mark(lambda E, dti=dti, nb=nb: E.matmul(
                            out=psM[:, 0:512],
                            lhsT=wdt_bf[:, dti * 128:dti * 128 + 128],
                            rhs=dtraw_b[:, nb * 512:(nb + 1) * 512],
                            start=True, stop=True))
                        wp("scalar", mp)

                        def sp1(E, dti=dti, nb=nb):
                            col = bdt_t[:, dti:dti + 1]
                            seg = dt_f[:, dti * C + nb * 512:dti * C + nb * 512 + 512]
                            E.activation(out=seg, in_=psM[:, 0:512], func=AF.Relu,
                                         bias=col)
                            return E.activation(out=sm_scr[:, 0:512],
                                                in_=psM[:, 0:512],
                                                func=AF.Abs, bias=col)
                        ma = s_mark(sp1)
                        tp.wait("scalar", "ac", ma)
                        ma = s_mark(lambda E: E.activation(
                            out=sm_scr[:, 0:512], in_=sm_scr[:, 0:512],
                            func=AF.Exp, scale=-1.0))
                        tp.wait("scalar", "ac", ma)
                        ma = s_mark(lambda E: E.activation(
                            out=sm_scr[:, 0:512], in_=sm_scr[:, 0:512],
                            func=AF.Ln, bias=1.0))
                        wa("tensor", ma)
                        wa("vector", ma)
                        mv = v_mark(lambda E, dti=dti, nb=nb: E.tensor_tensor(
                            out=dt_f[:, dti * C + nb * 512:dti * C + nb * 512 + 512],
                            in0=dt_f[:, dti * C + nb * 512:dti * C + nb * 512 + 512],
                            in1=sm_scr[:, 0:512], op=ALU.add))
                        wv("scalar", mv)
                if k > 0:
                    wg("vector", git0)
                    wp("vector", tp.cur("pe"))
                for dti in range(4):
                    v_op(lambda E, dti=dti: E.tensor_tensor(
                        out=u_f[:, dti * C:(dti + 1) * C],
                        in0=dt_f[:, dti * C:(dti + 1) * C],
                        in1=xn_f[:, dti * C:(dti + 1) * C], op=ALU.mult))
                    v_op(lambda E, dti=dti: E.tensor_scalar(
                        out=ya_f[:, dti * C:(dti + 1) * C],
                        in0=xn_f[:, dti * C:(dti + 1) * C],
                        scalar1=dpar_t[:, dti:dti + 1], scalar2=None, op0=ALU.mult))
                mv_u = v_mark()
                wv("scalar", mv_u)
                if DEBUG_STOP == "P2h" and k == 1:
                    tp.op("vector", lambda E: E.tensor_copy(
                        dbx_sl[:, C:2 * C], dtraw_b[:]))
                    mv_dbg = tp.mark("vector", "dv")
                    wv("gpsimd", mv_dbg)
                    gstore(out_flat[0:128 * 64].rearrange("(a b) -> a b", a=128),
                           hstate[:])
                    gstore(out_flat[128 * 64:128 * 64 + 128 * 4096].rearrange(
                        "(a b) -> a b", a=128), dt_f[:])
                    gstore(out_flat[128 * 64 + 128 * 4096:
                                    128 * 64 + 128 * 4096 + 128 * 1024].rearrange(
                        "(a b) -> a b", a=128), dbx_sl[:, C:2 * C])
                    tp.wait("gpsimd", "st", tp.cur("st"))
                    raise _StopBuild

                pend_copy = [None]
                for n in range(NST):
                    def rep(E, n=n):
                        for nb in range(2):
                            for i, src in enumerate((bc_h, bc_l, bc_l2)):
                                E.matmul(out=psA[:, nb * 512:(nb + 1) * 512],
                                         lhsT=sel32[:, n * 128:(n + 1) * 128],
                                         rhs=src[:, nb * 512:(nb + 1) * 512],
                                         start=(i == 0), stop=(i == 2))
                            for i, src in enumerate((bc_h, bc_l, bc_l2)):
                                last = E.matmul(out=psB[:, nb * 512:(nb + 1) * 512],
                                         lhsT=sel32[:, (16 + n) * 128:(17 + n) * 128],
                                         rhs=src[:, nb * 512:(nb + 1) * 512],
                                         start=(i == 0), stop=(i == 2))
                        return last
                    if n == 0:
                        wv("tensor", mv_u)
                    else:
                        wv("tensor", tp.cur("dv"))
                    mp_rep = p_mark(rep)
                    for dti in range(4):
                        git = git0 + n * 4 + dti
                        sl = git % 2
                        if git >= 2:
                            wv("scalar", da_gate[git - 2])
                        ma_da = s_mark(lambda E, dti=dti, n=n, sl=sl: E.activation(
                            out=da_sl[:, sl * C:(sl + 1) * C],
                            in_=dt_f[:, dti * C:(dti + 1) * C],
                            func=AF.Exp,
                            scale=aneg[:, dti * 16 + n:dti * 16 + n + 1]))
                        wa("vector", ma_da)
                        wp("vector", mp_rep)
                        if git >= 2:
                            wg("vector", git - 1)

                        def scan_body(E, dti=dti, n=n, sl=sl,
                                      prev=pend_copy[0]):
                            E.tensor_tensor(out=dbx_sl[:, sl * C:(sl + 1) * C],
                                            in0=u_f[:, dti * C:(dti + 1) * C],
                                            in1=psA[:, 0:1024], op=ALU.mult)
                            if prev is not None:
                                pd, pn, psl = prev
                                E.tensor_copy(
                                    hstate[:, pd * 16 + pn:pd * 16 + pn + 1],
                                    h_sl[:, (psl + 1) * C - 1:(psl + 1) * C])
                            return E.tensor_tensor_scan(
                                out=h_sl[:, sl * C:(sl + 1) * C],
                                data0=da_sl[:, sl * C:(sl + 1) * C],
                                data1=dbx_sl[:, sl * C:(sl + 1) * C],
                                initial=hstate[:, dti * 16 + n:dti * 16 + n + 1],
                                op0=ALU.mult, op1=ALU.add)
                        pend_copy[0] = (dti, n, sl)
                        mv_scan = v_mark(lambda E, f=scan_body, dti=dti, sl=sl: (
                            f(E),
                            E.tensor_tensor(
                                out=hc_sl[:, sl * C:(sl + 1) * C],
                                in0=h_sl[:, sl * C:(sl + 1) * C],
                                in1=psB[:, 0:1024], op=ALU.mult))[-1])
                        da_gate[git] = mv_scan
                        wv("gpsimd", mv_scan)
                        g_mark(lambda E, dti=dti, sl=sl: E.tensor_tensor(
                            out=ya_f[:, dti * C:(dti + 1) * C],
                            in0=ya_f[:, dti * C:(dti + 1) * C],
                            in1=hc_sl[:, sl * C:(sl + 1) * C], op=ALU.add))
                def flush_copy(E, prev=pend_copy[0]):
                    pd, pn, psl = prev
                    return E.tensor_copy(
                        hstate[:, pd * 16 + pn:pd * 16 + pn + 1],
                        h_sl[:, (psl + 1) * C - 1:(psl + 1) * C])
                v_mark(flush_copy)
                pend_copy[0] = None
                wg("tensor", git0 + NST * 4)
                for dti in range(4):
                    for g in range(2):
                        for j in range(4):
                            tt = g * 4 + j
                            p_op(lambda E, dti=dti, tt=tt, j=j: E.transpose(
                                out=psT[:, j * 128:(j + 1) * 128],
                                in_=ya_f[:, dti * C + tt * 128:dti * C + tt * 128 + 128],
                                identity=ident_f[:]))
                        mp = p_mark()
                        wp("scalar", mp)

                        def evac_y(E, dti=dti, g=g):
                            for j in range(4):
                                tt = g * 4 + j
                                last = E.activation(
                                    out=scr[:, tt * DQ + dti * 128:tt * DQ + dti * 128 + 128],
                                    in_=psT[:, j * 128:(j + 1) * 128], func=AF.Copy)
                            return last
                        ma = s_mark(evac_y)
                        wa("tensor", ma)
                ma_yT = tp.cur("ac")
                wa("vector", ma_yT)
                for tt in range(NTT):
                    v_op(lambda E, k=k, tt=tt: E.reduce_sum(
                        st1[:, k * 24 + tt:k * 24 + tt + 1],
                        scr[:, tt * DQ:(tt + 1) * DQ], axis=AX.X))
                    v_op(lambda E, k=k, tt=tt: E.tensor_tensor(
                        out=sm_scr[:, 0:512],
                        in0=scr[:, tt * DQ:(tt + 1) * DQ],
                        in1=xT_t[:, tt * DQ:(tt + 1) * DQ], op=ALU.mult))
                    v_op(lambda E, k=k, tt=tt: E.reduce_sum(
                        st1[:, k * 24 + 16 + tt:k * 24 + 17 + tt],
                        sm_scr[:, 0:512], axis=AX.X))
                mv_st = v_mark()
                wv("scalar", mv_st)
                for tt in range(NTT):
                    s_op(lambda E, k=k, tt=tt: E.activation(
                        out=sm_scr[:, 0:512],
                        in_=scr[:, tt * DQ:(tt + 1) * DQ], func=AF.Square,
                        accum_out=st1[:, k * 24 + 8 + tt:k * 24 + 9 + tt]))
                ma_st = s_mark()
                wa("gpsimd", ma_st)
                wv("gpsimd", mv_st)
                gstore(yb_chunks[k], scr_v)
            wv("gpsimd", tp.cur("dv"))
            wa("gpsimd", tp.cur("ac"))
            gstore(cc1_in.ap(), st1[:])
            wst("gpsimd")
            cc_s1 = collective("AllReduce", ALU.add, cc1_in, cc1_out)
            if DEBUG_STOP == "P2":
                tp.wait("gpsimd", "cc", cc_s1)
                maybe_stop("P2", [ybuf])
            wc("sync", cc_s1)
            sload(st1[:], cc1_out.ap())

            for k in range(NCH):
                tp.wait("sync", "dv", tp.cur("dv"))
                tp.wait("sync", "ac", tp.cur("ac"))
                tp.wait("sync", "st", tp.cur("st"))
                sload(xT_v, xT_chunks[k])
                sload(scr_v, yb_chunks[k])
                wl("vector")

                mu1 = lncol[:, 0:8]
                var1 = lncol[:, 8:16]
                rr = lncol[:, 16:24]
                m1 = lncol[:, 24:32]
                t1 = lncol[:, 32:40]
                t2 = lncol[:, 40:48]
                mv = vseq(
                    lambda E, k=k: E.tensor_scalar_mul(
                        mu1, st1[:, k * 24:k * 24 + 8], 1.0 / D),
                    lambda E, k=k: E.tensor_scalar_mul(
                        var1, st1[:, k * 24 + 8:k * 24 + 16], 1.0 / D),
                    lambda E: E.tensor_tensor(out=lncol[:, 56:64], in0=mu1,
                                              in1=mu1, op=ALU.mult),
                    lambda E: E.tensor_tensor(out=var1, in0=var1,
                                              in1=lncol[:, 56:64], op=ALU.subtract),
                    lambda E: E.tensor_copy(rr, var1),
                    lambda E: E.tensor_scalar_add(rr, rr, EPS),
                )
                wv("scalar", mv)
                ma = s_mark(lambda E: E.activation(out=lncol[:, 16:24],
                                                   in_=lncol[:, 16:24], func=AF.Sqrt))
                wa("vector", ma)
                mv = vseq(
                    lambda E: E.reciprocal(rr, rr),
                    lambda E: E.tensor_tensor(out=m1, in0=mu1, in1=rr, op=ALU.mult),
                    lambda E: E.tensor_scalar_mul(m1, m1, -1.0),
                    lambda E, k=k: E.tensor_tensor(
                        out=t1, in0=mu1, in1=st1[:, k * 24:k * 24 + 8], op=ALU.mult),
                    lambda E, k=k: E.tensor_tensor(
                        out=t1, in0=st1[:, k * 24 + 16:k * 24 + 24], in1=t1,
                        op=ALU.subtract),
                    lambda E: E.tensor_tensor(out=t1, in0=t1, in1=rr, op=ALU.mult),
                    lambda E: E.tensor_scalar_mul(t1, t1, 2.0),
                    lambda E: E.tensor_tensor(out=t2, in0=var1, in1=rr, op=ALU.mult),
                    lambda E: E.tensor_tensor(out=t2, in0=t2, in1=rr, op=ALU.mult),
                    lambda E: E.tensor_scalar_mul(t2, t2, float(D)),
                    lambda E: E.tensor_tensor(out=t1, in0=t1, in1=t2, op=ALU.add),
                    lambda E, k=k: E.tensor_tensor(
                        out=t1, in0=st0[:, k * 16 + 8:k * 16 + 16], in1=t1,
                        op=ALU.add),
                    lambda E: E.tensor_scalar_mul(t1, t1, 1.0 / D),
                    lambda E, k=k: E.tensor_tensor(
                        out=t2, in0=mu0[:, k * 8:(k + 1) * 8],
                        in1=mu0[:, k * 8:(k + 1) * 8], op=ALU.mult),
                    lambda E: E.tensor_tensor(out=t1, in0=t1, in1=t2,
                                              op=ALU.subtract),
                    lambda E: E.tensor_scalar_add(t1, t1, EPS),
                )
                wv("scalar", mv)
                ma = s_mark(lambda E: E.activation(out=lncol[:, 32:40],
                                                   in_=lncol[:, 32:40], func=AF.Sqrt))
                wa("vector", ma)
                vseq(
                    lambda E: E.reciprocal(t1, t1),
                    lambda E, k=k: E.tensor_tensor(
                        out=t2, in0=mu0[:, k * 8:(k + 1) * 8], in1=t1, op=ALU.mult),
                    lambda E: E.tensor_scalar_mul(t2, t2, -1.0),
                )
                for tt in range(NTT):
                    def x1_body(E, k=k, tt=tt):
                        seg_y = scr[:, tt * DQ:(tt + 1) * DQ]
                        seg_x = xT_t[:, tt * DQ:(tt + 1) * DQ]
                        E.tensor_scalar(out=seg_y, in0=seg_y,
                                        scalar1=lncol[:, 16 + tt:17 + tt],
                                        scalar2=lncol[:, 24 + tt:25 + tt],
                                        op0=ALU.mult, op1=ALU.add)
                        E.tensor_tensor(out=seg_y, in0=seg_y, in1=seg_x, op=ALU.add)
                        return E.tensor_scalar(
                            out=x1n_b[:, tt * DQ:(tt + 1) * DQ], in0=seg_y,
                            scalar1=lncol[:, 32 + tt:33 + tt],
                            scalar2=lncol[:, 40 + tt:41 + tt],
                            op0=ALU.mult, op1=ALU.add)
                    v_mark(x1_body)
                mv_x1 = tp.cur("dv")
                if DEBUG_STOP == "P3s" and k == 0:
                    wv("gpsimd", mv_x1)
                    gstore(out_flat[0:128 * 64].rearrange("(a b) -> a b", a=128),
                           lncol[:])
                    gstore(out_flat[128 * 64:128 * 64 + 128 * 4096].rearrange(
                        "(a b) -> a b", a=128), scr[:])
                    gstore(out_flat[128 * 64 + 128 * 4096:
                                    128 * 64 + 128 * 4096 + 128 * 24 * NCH].rearrange(
                        "(a b) -> a b", a=128), st1[:])
                    tp.wait("gpsimd", "st", tp.cur("st"))
                    raise _StopBuild
                wv("gpsimd", mv_x1)
                gstore(x1_chunks[k], scr_v)
                wv("tensor", mv_x1)
                for dti in range(4):
                    for tt in range(NTT):
                        p_op(lambda E, dti=dti, tt=tt: E.transpose(
                            out=psX[:, tt * 128:(tt + 1) * 128],
                            in_=x1n_b[:, tt * DQ + dti * 128:tt * DQ + dti * 128 + 128],
                            identity=ident_b[:]))
                    mp = p_mark()
                    wp("scalar", mp)
                    ma = s_mark(lambda E, dti=dti: E.activation(
                        out=up_b[:, dti * C:(dti + 1) * C], in_=psX[:, 0:1024],
                        func=AF.Copy))
                    wa("tensor", ma)
                    wa("gpsimd", ma)
                    gstore(agx_in.ap()[dti * 128:(dti + 1) * 128, k * C:(k + 1) * C],
                           up_b[:, dti * C:(dti + 1) * C])
            wst("gpsimd")
            cc_agx = collective("AllGather", ALU.bypass, agx_in, agx_out)
            if DEBUG_STOP == "P3":
                tp.wait("gpsimd", "cc", cc_agx)
                maybe_stop("P3", [x1buf])

            mm_hist = []
            tp.wait("sync", "cc", cc_agx)
            for k in range(NCH):
                for nb in range(2):
                    for kt in range(16):
                        sl = kt % 2
                        if len(mm_hist) >= 2:
                            wp("sync", mm_hist[-2])
                        lv = sload(kt_a[:, sl * C:sl * C + 512],
                                   agx_out.ap()[kt * 128:(kt + 1) * 128,
                                                k * C + nb * 512:k * C + nb * 512 + 512])

                        def mm_up(E, kt=kt, sl=sl):
                            for mt in range(4):
                                ps = (psA, psA, psB, psB)[mt]
                                last = E.matmul(
                                    out=ps[:, (mt % 2) * 512:(mt % 2) * 512 + 512],
                                    lhsT=wup_bf[:, kt * DQ + mt * 128:kt * DQ + mt * 128 + 128],
                                    rhs=kt_a[:, sl * C:sl * C + 512],
                                    start=(kt == 0), stop=(kt == 15))
                            return last
                        tp.wait("tensor", "ld", lv)
                        mm_hist.append(p_mark(mm_up))
                    wp("scalar", mm_hist[-1])

                    def evac_up(E, nb=nb):
                        for mt in range(4):
                            ps = (psA, psA, psB, psB)[mt]
                            last = E.activation(
                                out=up_b[:, mt * C + nb * 512:mt * C + nb * 512 + 512],
                                in_=ps[:, (mt % 2) * 512:(mt % 2) * 512 + 512],
                                func=AF.Relu)
                        return last
                    ma = s_mark(evac_up)
                    wa("tensor", ma)
                    wa("gpsimd", ma)
                    for mt in range(4):
                        gstore(agu_in.ap()[mt * 128:(mt + 1) * 128,
                                           k * C + nb * 512:k * C + nb * 512 + 512],
                               up_b[:, mt * C + nb * 512:mt * C + nb * 512 + 512])
            wst("gpsimd")
            cc_agu = collective("AllGather", ALU.bypass, agu_in, agu_out)
            if DEBUG_STOP == "P4":
                tp.wait("gpsimd", "cc", cc_agu)
                maybe_stop("P4", [agu_in])

            p4_last_mm = mm_hist[-1]
            mm_hist = []
            tp.wait("sync", "cc", cc_agu)
            wp("sync", p4_last_mm)
            for k in range(NCH):
                for nb in range(2):
                    for kt in range(16):
                        sl = kt % 2
                        if len(mm_hist) >= 2:
                            wp("sync", mm_hist[-2])
                        sload(kt_a[:, sl * C:sl * C + 512],
                              agu_out.ap()[kt * 128:(kt + 1) * 128,
                                           k * C + nb * 512:k * C + nb * 512 + 512])
                        lv2 = sload(kt_b[:, sl * C:sl * C + 512],
                                    agx_out.ap()[kt * 128:(kt + 1) * 128,
                                                 k * C + nb * 512:k * C + nb * 512 + 512])

                        def mm_f(E, kt=kt, sl=sl):
                            for mt in range(4):
                                ps = (psA, psA, psB, psB)[mt]
                                E.matmul(
                                    out=ps[:, (mt % 2) * 512:(mt % 2) * 512 + 512],
                                    lhsT=wdn_bf[:, kt * DQ + mt * 128:kt * DQ + mt * 128 + 128],
                                    rhs=kt_a[:, sl * C:sl * C + 512],
                                    start=(kt == 0), stop=False)
                                last = E.matmul(
                                    out=ps[:, (mt % 2) * 512:(mt % 2) * 512 + 512],
                                    lhsT=wex_bf[:, kt * DQ + mt * 128:kt * DQ + mt * 128 + 128],
                                    rhs=kt_b[:, sl * C:sl * C + 512],
                                    start=False, stop=(kt == 15))
                            return last
                        tp.wait("tensor", "ld", lv2)
                        mm_hist.append(p_mark(mm_f))
                    wp("vector", mm_hist[-1])

                    def evac_f(E, nb=nb):
                        for mt in range(4):
                            ps = (psA, psA, psB, psB)[mt]
                            last = E.tensor_copy(
                                u_f[:, mt * C + nb * 512:mt * C + nb * 512 + 512],
                                ps[:, (mt % 2) * 512:(mt % 2) * 512 + 512])
                        return last
                    mv = v_mark(evac_f)
                    wv("tensor", mv)
                mv_f = tp.cur("dv")
                wv("tensor", mv_f)
                wst("tensor")
                for dti in range(4):
                    for g in range(2):
                        for j in range(4):
                            tt = g * 4 + j
                            p_op(lambda E, dti=dti, tt=tt, j=j: E.transpose(
                                out=psT[:, j * 128:(j + 1) * 128],
                                in_=u_f[:, dti * C + tt * 128:dti * C + tt * 128 + 128],
                                identity=ident_f[:]))
                        mp = p_mark()
                        wp("scalar", mp)

                        def evac_fT(E, dti=dti, g=g):
                            for j in range(4):
                                tt = g * 4 + j
                                last = E.activation(
                                    out=scr[:, tt * DQ + dti * 128:tt * DQ + dti * 128 + 128],
                                    in_=psT[:, j * 128:(j + 1) * 128], func=AF.Copy)
                            return last
                        ma = s_mark(evac_fT)
                        wa("tensor", ma)
                ma_fT = tp.cur("ac")
                wa("vector", ma_fT)
                for tt in range(NTT):
                    v_op(lambda E, k=k, tt=tt: E.reduce_sum(
                        st3[:, k * 16 + tt:k * 16 + tt + 1],
                        scr[:, tt * DQ:(tt + 1) * DQ], axis=AX.X))
                mv_s3 = v_mark()
                wv("scalar", mv_s3)
                for tt in range(NTT):
                    s_op(lambda E, k=k, tt=tt: E.activation(
                        out=sm_scr[:, 0:512],
                        in_=scr[:, tt * DQ:(tt + 1) * DQ], func=AF.Square,
                        accum_out=st3[:, k * 16 + 8 + tt:k * 16 + 9 + tt]))
                ma_s3 = s_mark()
                wa("gpsimd", ma_s3)
                wv("gpsimd", mv_s3)
                gstore(fb_chunks[k], scr_v)
            wv("gpsimd", tp.cur("dv"))
            wa("gpsimd", tp.cur("ac"))
            gstore(cc3_in.ap(), st3[:])
            wst("gpsimd")
            cc_s3 = collective("AllReduce", ALU.add, cc3_in, cc3_out)
            if DEBUG_STOP == "P5a":
                tp.wait("gpsimd", "cc", cc_s3)
                maybe_stop("P5a", [fbuf])
            wc("sync", cc_s3)
            sload(st3[:], cc3_out.ap())

            for k in range(NCH):
                tp.wait("sync", "dv", tp.cur("dv"))
                tp.wait("sync", "st", tp.cur("st"))
                sload(xT_v, x1_chunks[k])
                sload(scr_v, fb_chunks[k])
                wl("vector")

                mv = vseq(
                    lambda E, k=k: E.tensor_scalar_mul(
                        lncol[:, 0:8], st3[:, k * 16:k * 16 + 8], 1.0 / D),
                    lambda E, k=k: E.tensor_scalar_mul(
                        lncol[:, 8:16], st3[:, k * 16 + 8:k * 16 + 16], 1.0 / D),
                    lambda E: E.tensor_tensor(out=lncol[:, 56:64], in0=lncol[:, 0:8],
                                              in1=lncol[:, 0:8], op=ALU.mult),
                    lambda E: E.tensor_tensor(out=lncol[:, 8:16], in0=lncol[:, 8:16],
                                              in1=lncol[:, 56:64], op=ALU.subtract),
                    lambda E: E.tensor_copy(lncol[:, 16:24], lncol[:, 8:16]),
                    lambda E: E.tensor_scalar_add(lncol[:, 16:24], lncol[:, 16:24],
                                                  EPS),
                )
                wv("scalar", mv)
                ma = s_mark(lambda E: E.activation(out=lncol[:, 16:24],
                                                   in_=lncol[:, 16:24], func=AF.Sqrt))
                wa("vector", ma)
                vseq(
                    lambda E: E.reciprocal(lncol[:, 16:24], lncol[:, 16:24]),
                    lambda E: E.tensor_tensor(out=lncol[:, 24:32], in0=lncol[:, 0:8],
                                              in1=lncol[:, 16:24], op=ALU.mult),
                    lambda E: E.tensor_scalar_mul(lncol[:, 24:32], lncol[:, 24:32],
                                                  -1.0),
                )

                def outb(E, k=k):
                    rr = lncol[:, 16:24]
                    m3 = lncol[:, 24:32]
                    for tt in range(NTT):
                        seg_f = scr[:, tt * DQ:(tt + 1) * DQ]
                        E.tensor_scalar(out=seg_f, in0=seg_f,
                                        scalar1=rr[:, tt:tt + 1],
                                        scalar2=m3[:, tt:tt + 1],
                                        op0=ALU.mult, op1=ALU.add)
                        last = E.tensor_tensor(out=seg_f, in0=seg_f,
                                        in1=xT_t[:, tt * DQ:(tt + 1) * DQ], op=ALU.add)
                    return last
                mv = v_mark(outb)
                wv("gpsimd", mv)
                gstore(out_hbm.rearrange("(k a p) d -> k p a d", p=128, a=NTT)[k],
                       scr_v)
            final_st = tp.cur("st")
            tp.wait("gpsimd", "st", final_st)


        try:
            emit()
        except _StopBuild:
            pass

        @block.sync
        def _(E):
            tp.replay("sync", E, SEMS)

        @block.gpsimd
        def _(E):
            tp.replay("gpsimd", E, SEMS)

        @block.vector
        def _(E):
            tp.replay("vector", E, SEMS)

        @block.scalar
        def _(E):
            tp.replay("scalar", E, SEMS)

        @block.tensor
        def _(E):
            tp.replay("tensor", E, SEMS)

    return nc


def _per_core_inputs(inputs, core):
    g, q = core // 4, core % 4
    dq = slice(DQ * q, DQ * (q + 1))
    x = np.asarray(inputs["L0_attn_x"])
    W_x = np.asarray(inputs["L0_attn_W_x"])
    W_dt = np.asarray(inputs["L0_attn_W_dt"])
    b_dt = np.asarray(inputs["L0_attn_b_dt"])
    A_log = np.asarray(inputs["L0_attn_A_log"])
    D_param = np.asarray(inputs["L0_attn_D_param"])
    W_up = np.asarray(inputs["L0_ffn_W_up"])
    W_down = np.asarray(inputs["L0_ffn_W_down"])
    W_expert = np.asarray(inputs["L0_ffn_W_expert"])

    sel = np.zeros((32, 32 * 128), np.float32)
    for n in range(32):
        sel[n, n * 128:(n + 1) * 128] = 1.0

    return {
        "xT_in": np.ascontiguousarray(x[g][:, dq]),
        "wx_in": np.ascontiguousarray(W_x.T[dq, :]),
        "wdt_in": np.ascontiguousarray(W_dt[dq, :].T),
        "bdt_in": np.ascontiguousarray(b_dt[dq].reshape(4, 128).T),
        "alog_in": np.ascontiguousarray(
            A_log[dq].reshape(4, 128, NST).transpose(1, 0, 2).reshape(128, 64)),
        "dpar_in": np.ascontiguousarray(D_param[dq].reshape(4, 128).T),
        "wup_in": np.ascontiguousarray(W_up[dq, :].T),
        "wdn_in": np.ascontiguousarray(W_down[dq, :].T),
        "wex_in": np.ascontiguousarray(W_expert[:, dq]),
        "ident_in": np.eye(128, dtype=np.float32),
        "sel32_in": sel,
    }


def kernel(**inputs):
    nc = build()
    in_maps = [_per_core_inputs(inputs, c) for c in range(8)]
    res = run_bass_kernel_spmd(nc, in_maps, core_ids=list(range(8)))
    out = np.empty((B, T, D), np.float32)
    for c in range(8):
        g, q = c // 4, c % 4
        out[g][:, DQ * q:DQ * (q + 1)] = res.results[c]["outT"]
    return out


if __name__ == "__main__":
    nc = build()
    print("build ok; instrs:",
          sum(1 for _ in nc.all_instructions()) if hasattr(nc, "all_instructions") else "?")


# version 08
# speedup vs baseline: 1.2984x; 1.0628x over previous
import numpy as np
import concourse.bass as bass
import concourse.mybir as mybir
from concourse.bass_utils import run_bass_kernel_spmd

F32 = mybir.dt.float32
BF16 = mybir.dt.bfloat16
ALU = mybir.AluOpType
AF = mybir.ActivationFunctionType
AX = mybir.AxisListType

B = 2
T = 8192
D = 2048
DQ = 512
NST = 16
RK = 128
C = 1024
NCH = T // C
NTT = C // 128
EPS = 1e-5
RG = [[0, 1, 2, 3], [4, 5, 6, 7]]

ENGINES = ("sync", "gpsimd", "vector", "scalar", "tensor")

YA_ON_GP = True
DEBUG_STOP = None


class _StopBuild(Exception):
    pass


class Tape:

    def __init__(self):
        self.q = {e: [] for e in ENGINES}
        self.cnt = {}

    def op(self, eng, fn):
        self.q[eng].append([fn, None, 0])

    def mark(self, eng, sem, amt=1):
        ent = self.q[eng][-1]
        assert ent[1] is None, "op already carries an inc"
        self.cnt[sem] = self.cnt.get(sem, 0) + amt
        ent[1], ent[2] = sem, amt
        return self.cnt[sem]

    def inc(self, eng, sem, fn, amt):
        self.op(eng, fn)
        return self.mark(eng, sem, amt)

    def wait(self, eng, sem, val):
        if val is None or val <= 0:
            return
        self.q[eng].append([("wait", sem, val), None, 0])

    def cur(self, sem):
        return self.cnt.get(sem, 0)

    def replay(self, eng, E, sems):
        for fn, sem, amt in self.q[eng]:
            if isinstance(fn, tuple):
                E.wait_ge(sems[fn[1]], fn[2])
                continue
            inst = fn(E)
            if sem is not None:
                assert inst is not None, f"{eng} op with inc returned no inst"
                inst.then_inc(sems[sem], amt)


def build():
    nc = bass.Bass("TRN2", target_bir_lowering=False)

    xT_in = nc.dram_tensor("xT_in", [T, DQ], F32, kind="ExternalInput").ap()
    wx_in = nc.dram_tensor("wx_in", [DQ, 160], F32, kind="ExternalInput").ap()
    wdt_in = nc.dram_tensor("wdt_in", [RK, DQ], F32, kind="ExternalInput").ap()
    bdt_in = nc.dram_tensor("bdt_in", [128, 4], F32, kind="ExternalInput").ap()
    alog_in = nc.dram_tensor("alog_in", [128, 64], F32, kind="ExternalInput").ap()
    dpar_in = nc.dram_tensor("dpar_in", [128, 4], F32, kind="ExternalInput").ap()
    wup_in = nc.dram_tensor("wup_in", [D, DQ], F32, kind="ExternalInput").ap()
    wdn_in = nc.dram_tensor("wdn_in", [D, DQ], F32, kind="ExternalInput").ap()
    wex_in = nc.dram_tensor("wex_in", [D, DQ], F32, kind="ExternalInput").ap()
    ident_in = nc.dram_tensor("ident_in", [128, 128], F32, kind="ExternalInput").ap()
    sel32_in = nc.dram_tensor("sel32_in", [32, 32 * 128], F32, kind="ExternalInput").ap()
    out_hbm = nc.dram_tensor("outT", [T, DQ], F32, kind="ExternalOutput").ap()

    cc0_in = nc.dram_tensor("cc0_in", [128, 16 * NCH], F32, kind="Internal")
    cc0_out = nc.dram_tensor("cc0_out", [128, 16 * NCH], F32, kind="Internal")
    ccxz_in = nc.dram_tensor("ccxz_in", [160, T], F32, kind="Internal")
    ccxz_out = nc.dram_tensor("ccxz_out", [160, T], F32, kind="Internal")
    cc1_in = nc.dram_tensor("cc1_in", [128, 24 * NCH], F32, kind="Internal")
    cc1_out = nc.dram_tensor("cc1_out", [128, 24 * NCH], F32, kind="Internal")
    cc3_in = nc.dram_tensor("cc3_in", [128, 16 * NCH], F32, kind="Internal")
    cc3_out = nc.dram_tensor("cc3_out", [128, 16 * NCH], F32, kind="Internal")
    agx_in = nc.dram_tensor("agx_in", [DQ, T], BF16, kind="Internal")
    agx_out = nc.dram_tensor("agx_out", [D, T], BF16, kind="Internal")
    agu_in = nc.dram_tensor("agu_in", [DQ, T], BF16, kind="Internal")
    agu_out = nc.dram_tensor("agu_out", [D, T], BF16, kind="Internal")
    ybuf = nc.dram_tensor("ybuf", [T, DQ], F32, kind="Internal")
    x1buf = nc.dram_tensor("x1buf", [T, DQ], F32, kind="Internal")
    fbuf = nc.dram_tensor("fbuf", [T, DQ], F32, kind="Internal")

    xT_tiles = xT_in.rearrange("(n p) d -> n p d", p=128)
    yb_tiles = ybuf.ap().rearrange("(n p) d -> n p d", p=128)
    x1_tiles = x1buf.ap().rearrange("(n p) d -> n p d", p=128)
    fb_tiles = fbuf.ap().rearrange("(n p) d -> n p d", p=128)
    out_tiles = out_hbm.rearrange("(n p) d -> n p d", p=128)

    tp = Tape()

    from contextlib import ExitStack
    with ExitStack() as ctx:
        wup_bf = ctx.enter_context(nc.sbuf_tensor("wup_bf", [128, 16 * DQ], BF16))
        wdn_bf = ctx.enter_context(nc.sbuf_tensor("wdn_bf", [128, 16 * DQ], BF16))
        wex_bf = ctx.enter_context(nc.sbuf_tensor("wex_bf", [128, 16 * DQ], BF16))
        wx_bf = ctx.enter_context(nc.sbuf_tensor("wx_bf", [128, 4 * 160], BF16))
        wdt_bf = ctx.enter_context(nc.sbuf_tensor("wdt_bf", [128, DQ], BF16))
        bdt_t = ctx.enter_context(nc.sbuf_tensor("bdt_t", [128, 4], F32))
        aneg = ctx.enter_context(nc.sbuf_tensor("aneg", [128, 64], F32))
        dpar_t = ctx.enter_context(nc.sbuf_tensor("dpar_t", [128, 4], F32))
        ident_f = ctx.enter_context(nc.sbuf_tensor("ident_f", [128, 128], F32))
        ident_b = ctx.enter_context(nc.sbuf_tensor("ident_b", [128, 128], BF16))
        sel32 = ctx.enter_context(nc.sbuf_tensor("sel32", [32, 32 * 128], BF16))
        st0 = ctx.enter_context(nc.sbuf_tensor("st0", [128, 16 * NCH], F32))
        mu0 = ctx.enter_context(nc.sbuf_tensor("mu0", [128, 8 * NCH], F32))
        r0c = ctx.enter_context(nc.sbuf_tensor("r0c", [128, 8 * NCH], F32))
        st1 = ctx.enter_context(nc.sbuf_tensor("st1", [128, 24 * NCH], F32))
        st3 = ctx.enter_context(nc.sbuf_tensor("st3", [128, 16 * NCH], F32))
        lncol = ctx.enter_context(nc.sbuf_tensor("lncol", [128, 64], F32))
        xT_t = ctx.enter_context(nc.sbuf_tensor("xT_t", [128, NTT * DQ], F32))
        scr = ctx.enter_context(nc.sbuf_tensor("scr", [128, NTT * DQ], F32))
        pool = ctx.enter_context(nc.sbuf_tensor("pool", [128, 4 * C], F32))
        dt_f = ctx.enter_context(nc.sbuf_tensor("dt_f", [128, 4 * C], F32))
        u_f = ctx.enter_context(nc.sbuf_tensor("u_f", [128, 4 * C], F32))
        ya_f = ctx.enter_context(nc.sbuf_tensor("ya_f", [128, 4 * C], F32))
        da_sl = ctx.enter_context(nc.sbuf_tensor("da_sl", [128, 2 * C], F32))
        dbx_sl = ctx.enter_context(nc.sbuf_tensor("dbx_sl", [128, 2 * C], F32))
        sm_scr = ctx.enter_context(nc.sbuf_tensor("sm_scr", [128, 512], F32))
        dtraw_b = ctx.enter_context(nc.sbuf_tensor("dtraw_b", [128, C], BF16))
        bc_t = ctx.enter_context(nc.sbuf_tensor("bc_t", [32, C], F32))
        bc_h = ctx.enter_context(nc.sbuf_tensor("bc_h", [32, C], BF16))
        bc_l = ctx.enter_context(nc.sbuf_tensor("bc_l", [32, C], BF16))
        bc_l2 = ctx.enter_context(nc.sbuf_tensor("bc_l2", [32, C], BF16))
        xz32 = ctx.enter_context(nc.sbuf_tensor("xz32", [32, C], F32))
        x1n_b = ctx.enter_context(nc.sbuf_tensor("x1n_b", [128, 4 * C], BF16))
        up_b = ctx.enter_context(nc.sbuf_tensor("up_b", [128, 4 * C], BF16))
        psA = ctx.enter_context(nc.psum_tensor("psA", [128, 1024], F32))
        psB = ctx.enter_context(nc.psum_tensor("psB", [128, 1024], F32))
        psM = ctx.enter_context(nc.psum_tensor("psM", [128, 512], F32))
        psN = ctx.enter_context(nc.psum_tensor("psN", [128, 512], F32))
        psT = ctx.enter_context(nc.psum_tensor("psT", [128, 512], F32))
        psF = ctx.enter_context(nc.psum_tensor("psF", [128, 512], F32))
        SEM_ld = ctx.enter_context(nc.semaphore("ld"))
        SEM_st = ctx.enter_context(nc.semaphore("st"))
        SEM_cc = ctx.enter_context(nc.semaphore("cc"))
        SEM_pe = ctx.enter_context(nc.semaphore("pe"))
        SEM_dv = ctx.enter_context(nc.semaphore("dv"))
        SEM_ac = ctx.enter_context(nc.semaphore("ac"))
        SEM_gp = ctx.enter_context(nc.semaphore("gp"))
        SEM_gl = ctx.enter_context(nc.semaphore("gl"))
        block = ctx.enter_context(nc.Block())
        SEMS = {"ld": SEM_ld, "st": SEM_st, "cc": SEM_cc, "pe": SEM_pe,
                "dv": SEM_dv, "ac": SEM_ac, "gp": SEM_gp, "gl": SEM_gl}

        hstate = lncol
        out_flat = out_hbm.rearrange("a b -> (a b)")
        xT_chunks = xT_in.rearrange("(k a p) d -> k p a d", p=128, a=NTT)
        yb_chunks = ybuf.ap().rearrange("(k a p) d -> k p a d", p=128, a=NTT)
        x1_chunks = x1buf.ap().rearrange("(k a p) d -> k p a d", p=128, a=NTT)
        fb_chunks = fbuf.ap().rearrange("(k a p) d -> k p a d", p=128, a=NTT)
        xT_v = xT_t[:].rearrange("p (a d) -> p a d", a=NTT)
        scr_v = scr[:].rearrange("p (a d) -> p a d", a=NTT)

        def maybe_stop(tag, bufs):
            if DEBUG_STOP != tag:
                return
            off = 0
            for t in bufs:
                n = int(np.prod(t.shape))
                gstore(out_flat[off:off + n], t.ap().rearrange("a b -> (a b)"))
                off += n
            tp.wait("gpsimd", "st", tp.cur("st"))
            raise _StopBuild

        xn_f = pool
        h_sl = pool[:, 0:2 * C]
        hc_sl = pool[:, 2 * C:4 * C]
        kt_a = x1n_b[:, 0:2 * C]
        kt_b = x1n_b[:, 2 * C:4 * C]

        def sload(dst, src):
            return tp.inc("sync", "ld",
                          lambda E, d=dst, s=src: E.dma_start(out=d, in_=s), 16)

        def gstore(dst, src):
            return tp.inc("gpsimd", "st",
                          lambda E, d=dst, s=src: E.dma_start(out=d, in_=s), 16)

        def collective(kind, op, cin, cout):
            return tp.inc("gpsimd", "cc",
                          lambda E, k=kind, o=op, a=cin, b=cout:
                          E.collective_compute(k, o, ins=[a.ap()], outs=[b.ap()],
                                               replica_groups=RG), 1)

        def v_op(fn):
            tp.op("vector", fn)

        def v_mark(fn=None):
            if fn is not None:
                tp.op("vector", fn)
            return tp.mark("vector", "dv")

        def vseq(*fns):
            mv = None
            for fn in fns:
                tp.op("vector", fn)
                mv = tp.mark("vector", "dv")
                tp.wait("vector", "dv", mv)
            return mv

        def s_op(fn):
            tp.op("scalar", fn)

        def s_mark(fn=None):
            if fn is not None:
                tp.op("scalar", fn)
            return tp.mark("scalar", "ac")

        def p_op(fn):
            tp.op("tensor", fn)

        def p_mark(fn=None):
            if fn is not None:
                tp.op("tensor", fn)
            return tp.mark("tensor", "pe")

        def g_mark(fn):
            tp.op("gpsimd", fn)
            return tp.mark("gpsimd", "gp")

        def wv(eng, val):
            tp.wait(eng, "dv", val)

        def wa(eng, val):
            tp.wait(eng, "ac", val)

        def wp(eng, val):
            tp.wait(eng, "pe", val)

        def wg(eng, val):
            tp.wait(eng, "gp", val)

        def wl(eng):
            tp.wait(eng, "ld", tp.cur("ld"))

        def wst(eng):
            tp.wait(eng, "st", tp.cur("st"))

        def wc(eng, val):
            tp.wait(eng, "cc", val)

        def emit():
            last_cast = 0
            for wi, (src, dstb) in enumerate(
                    [(wup_in, wup_bf), (wdn_in, wdn_bf), (wex_in, wex_bf)]):
                for g4 in range(4):
                    if last_cast:
                        tp.wait("sync", "dv", last_cast)
                    for j in range(4):
                        kt = g4 * 4 + j
                        sload(scr[:, j * DQ:(j + 1) * DQ],
                              src[kt * 128:(kt + 1) * 128, :])
                    wl("vector")
                    for j in range(4):
                        kt = g4 * 4 + j
                        v_op(lambda E, j=j, kt=kt, dstb=dstb: E.tensor_copy(
                            dstb[:, kt * DQ:(kt + 1) * DQ],
                            scr[:, j * DQ:(j + 1) * DQ]))
                    last_cast = v_mark()
            tp.wait("sync", "dv", last_cast)
            for j in range(4):
                sload(scr[:, j * 160:(j + 1) * 160], wx_in[j * 128:(j + 1) * 128, :])
            sload(scr[:, 640:1152], wdt_in)
            sload(bdt_t[:], bdt_in)
            sload(scr[:, 1152:1216], alog_in)
            sload(dpar_t[:], dpar_in)
            sload(ident_f[:], ident_in)
            sload(xT_t[0:32, 0:4096], sel32_in)
            wl("vector")
            for j in range(4):
                v_op(lambda E, j=j: E.tensor_copy(
                    wx_bf[:, j * 160:(j + 1) * 160], scr[:, j * 160:(j + 1) * 160]))
            v_op(lambda E: E.tensor_copy(wdt_bf[:], scr[:, 640:1152]))
            v_op(lambda E: E.tensor_copy(ident_b[:], ident_f[:]))
            v_op(lambda E: E.tensor_copy(sel32[:], xT_t[0:32, 0:4096]))
            mv0 = v_mark()
            wl("scalar")
            ma = s_mark(lambda E: E.activation(out=aneg[:], in_=scr[:, 1152:1216],
                                               func=AF.Exp))
            wa("vector", ma)
            v_op(lambda E: E.tensor_scalar_mul(aneg[:], aneg[:], -1.0))
            P0_dv = v_mark()

            for k in range(NCH):
                tp.wait("sync", "dv", tp.cur("dv"))
                tp.wait("sync", "ac", tp.cur("ac"))
                sload(xT_v, xT_chunks[k])
                wl("vector")
                for tt in range(NTT):
                    v_op(lambda E, k=k, tt=tt: E.reduce_sum(
                        st0[:, k * 16 + tt:k * 16 + tt + 1],
                        xT_t[:, tt * DQ:(tt + 1) * DQ], axis=AX.X))
                v_mark()
                wl("scalar")
                for tt in range(NTT):
                    s_op(lambda E, k=k, tt=tt: E.activation(
                        out=sm_scr[:, 0:512],
                        in_=xT_t[:, tt * DQ:(tt + 1) * DQ], func=AF.Square,
                        accum_out=st0[:, k * 16 + 8 + tt:k * 16 + 9 + tt]))
                s_mark()
            wv("gpsimd", tp.cur("dv"))
            wa("gpsimd", tp.cur("ac"))
            gstore(cc0_in.ap(), st0[:])
            wst("gpsimd")
            cc_v = collective("AllReduce", ALU.add, cc0_in, cc0_out)
            wc("sync", cc_v)
            sload(st0[:], cc0_out.ap())
            wl("vector")
            for k in range(NCH):
                mv = vseq(
                    lambda E, k=k: E.tensor_scalar_mul(
                        mu0[:, k * 8:(k + 1) * 8], st0[:, k * 16:k * 16 + 8],
                        1.0 / D),
                    lambda E, k=k: E.tensor_scalar_mul(
                        r0c[:, k * 8:(k + 1) * 8], st0[:, k * 16 + 8:k * 16 + 16],
                        1.0 / D),
                    lambda E, k=k: E.tensor_tensor(
                        out=lncol[:, 0:8], in0=mu0[:, k * 8:(k + 1) * 8],
                        in1=mu0[:, k * 8:(k + 1) * 8], op=ALU.mult),
                    lambda E, k=k: E.tensor_tensor(
                        out=r0c[:, k * 8:(k + 1) * 8],
                        in0=r0c[:, k * 8:(k + 1) * 8],
                        in1=lncol[:, 0:8], op=ALU.subtract),
                    lambda E, k=k: E.tensor_scalar_add(
                        r0c[:, k * 8:(k + 1) * 8], r0c[:, k * 8:(k + 1) * 8], EPS),
                )
            wv("scalar", mv)
            ma = s_mark(lambda E: E.activation(out=r0c[:], in_=r0c[:], func=AF.Sqrt))
            wa("vector", ma)
            mv = tp.inc("vector", "dv", lambda E: E.reciprocal(r0c[:], r0c[:]), 1)
            tp.wait("vector", "dv", mv)

            for k in range(NCH):
                tp.wait("sync", "dv", tp.cur("dv"))
                tp.wait("sync", "ac", tp.cur("ac"))
                sload(xT_v, xT_chunks[k])
                wl("vector")
                for tt in range(NTT):
                    v_op(lambda E, k=k, tt=tt: E.tensor_scalar(
                        out=u_f[:, tt * DQ:(tt + 1) * DQ],
                        in0=xT_t[:, tt * DQ:(tt + 1) * DQ],
                        scalar1=mu0[:, k * 8 + tt:k * 8 + tt + 1],
                        scalar2=r0c[:, k * 8 + tt:k * 8 + tt + 1],
                        op0=ALU.subtract, op1=ALU.mult))
                mv = v_mark()
                wv("tensor", mv)
                for dti in range(4):
                    for g in range(2):
                        for j in range(4):
                            tt = g * 4 + j
                            p_op(lambda E, dti=dti, tt=tt, j=j: E.transpose(
                                out=psT[:, j * 128:(j + 1) * 128],
                                in_=u_f[:, tt * DQ + dti * 128:tt * DQ + dti * 128 + 128],
                                identity=ident_f[:]))
                        mp = p_mark()
                        wp("scalar", mp)
                        ma = s_mark(lambda E, dti=dti, g=g: E.activation(
                            out=up_b[:, dti * C + g * 512:dti * C + g * 512 + 512],
                            in_=psT[:, 0:512], func=AF.Copy))
                        wa("tensor", ma)
                for nb in range(2):
                    def mm_xz(E, nb=nb):
                        for kt in range(4):
                            E.matmul(out=psM[:, 0:512],
                                     lhsT=wx_bf[:, kt * 160:kt * 160 + 128],
                                     rhs=up_b[:, kt * C + nb * 512:kt * C + nb * 512 + 512],
                                     start=(kt == 0), stop=(kt == 3))
                        for kt in range(4):
                            last = E.matmul(out=psN[0:32, 0:512],
                                     lhsT=wx_bf[:, kt * 160 + 128:kt * 160 + 160],
                                     rhs=up_b[:, kt * C + nb * 512:kt * C + nb * 512 + 512],
                                     start=(kt == 0), stop=(kt == 3))
                        return last
                    mp = p_mark(mm_xz)
                    wp("vector", mp)
                    v_op(lambda E, nb=nb: E.tensor_copy(
                        scr[:, nb * 512:(nb + 1) * 512], psM[:, 0:512]))
                    v_op(lambda E, nb=nb: E.tensor_copy(
                        xz32[:, nb * 512:(nb + 1) * 512], psN[0:32, 0:512]))
                    mv = v_mark()
                    wv("tensor", mv)
                    wv("gpsimd", mv)
                    gstore(ccxz_in.ap()[0:128, k * C + nb * 512:k * C + nb * 512 + 512],
                           scr[:, nb * 512:(nb + 1) * 512])
                gstore(ccxz_in.ap()[128:160, k * C:(k + 1) * C], xz32[:])
            wst("gpsimd")
            cc_xz = collective("AllReduce", ALU.add, ccxz_in, ccxz_out)
            if DEBUG_STOP == "P1b":
                tp.wait("gpsimd", "cc", cc_xz)
                maybe_stop("P1b", [ccxz_out, cc0_out])

            v_op(lambda E: E.memset(hstate[:], 0.0))
            v_mark()
            da_gate = {}
            for k in range(NCH):
                git0 = k * NST * 4
                tp.wait("sync", "dv", tp.cur("dv"))
                tp.wait("sync", "ac", tp.cur("ac"))
                tp.wait("sync", "gp", git0)
                if k == 0:
                    tp.wait("sync", "cc", cc_xz)
                sload(xT_v, xT_chunks[k])
                sload(dbx_sl[:, 0:C], ccxz_out.ap()[0:128, k * C:(k + 1) * C])
                sload(bc_t[:], ccxz_out.ap()[128:160, k * C:(k + 1) * C])
                wl("vector")
                if k == 0:
                    wc("vector", cc_xz)
                for tt in range(NTT):
                    v_op(lambda E, k=k, tt=tt: E.tensor_scalar(
                        out=scr[:, tt * DQ:(tt + 1) * DQ],
                        in0=xT_t[:, tt * DQ:(tt + 1) * DQ],
                        scalar1=mu0[:, k * 8 + tt:k * 8 + tt + 1],
                        scalar2=r0c[:, k * 8 + tt:k * 8 + tt + 1],
                        op0=ALU.subtract, op1=ALU.mult))
                v_op(lambda E: E.tensor_copy(bc_h[:], bc_t[:]))
                v_op(lambda E: E.tensor_copy(xz32[:], bc_h[:]))
                v_op(lambda E: E.tensor_tensor(out=xz32[:], in0=bc_t[:], in1=xz32[:],
                                               op=ALU.subtract))
                v_op(lambda E: E.tensor_copy(bc_l[:], xz32[:]))
                v_op(lambda E: E.tensor_copy(bc_t[:], bc_l[:]))
                v_op(lambda E: E.tensor_tensor(out=xz32[:], in0=xz32[:], in1=bc_t[:],
                                               op=ALU.subtract))
                v_op(lambda E: E.tensor_copy(bc_l2[:], xz32[:]))
                v_op(lambda E: E.tensor_copy(dtraw_b[:], dbx_sl[:, 0:C]))
                mv_pre = v_mark()
                wv("tensor", mv_pre)
                wg("scalar", git0)
                for dti in range(4):
                    for g in range(2):
                        for j in range(4):
                            tt = g * 4 + j
                            p_op(lambda E, dti=dti, tt=tt, j=j: E.transpose(
                                out=psT[:, j * 128:(j + 1) * 128],
                                in_=scr[:, tt * DQ + dti * 128:tt * DQ + dti * 128 + 128],
                                identity=ident_f[:]))
                        mp = p_mark()
                        wp("scalar", mp)
                        ma = s_mark(lambda E, dti=dti, g=g: E.activation(
                            out=xn_f[:, dti * C + g * 512:dti * C + g * 512 + 512],
                            in_=psT[:, 0:512], func=AF.Copy))
                        wa("tensor", ma)
                for dti in range(4):
                    for nb in range(2):
                        mp = p_mark(lambda E, dti=dti, nb=nb: E.matmul(
                            out=psM[:, 0:512],
                            lhsT=wdt_bf[:, dti * 128:dti * 128 + 128],
                            rhs=dtraw_b[:, nb * 512:(nb + 1) * 512],
                            start=True, stop=True))
                        wp("scalar", mp)

                        def sp1(E, dti=dti, nb=nb):
                            col = bdt_t[:, dti:dti + 1]
                            seg = dt_f[:, dti * C + nb * 512:dti * C + nb * 512 + 512]
                            E.activation(out=seg, in_=psM[:, 0:512], func=AF.Relu,
                                         bias=col)
                            return E.activation(out=sm_scr[:, 0:512],
                                                in_=psM[:, 0:512],
                                                func=AF.Abs, bias=col)
                        ma = s_mark(sp1)
                        tp.wait("scalar", "ac", ma)
                        ma = s_mark(lambda E: E.activation(
                            out=sm_scr[:, 0:512], in_=sm_scr[:, 0:512],
                            func=AF.Exp, scale=-1.0))
                        tp.wait("scalar", "ac", ma)
                        ma = s_mark(lambda E: E.activation(
                            out=sm_scr[:, 0:512], in_=sm_scr[:, 0:512],
                            func=AF.Ln, bias=1.0))
                        wa("tensor", ma)
                        wa("vector", ma)
                        mv = v_mark(lambda E, dti=dti, nb=nb: E.tensor_tensor(
                            out=dt_f[:, dti * C + nb * 512:dti * C + nb * 512 + 512],
                            in0=dt_f[:, dti * C + nb * 512:dti * C + nb * 512 + 512],
                            in1=sm_scr[:, 0:512], op=ALU.add))
                        wv("scalar", mv)
                if k > 0:
                    wg("vector", git0)
                    wp("vector", tp.cur("pe"))
                for dti in range(4):
                    v_op(lambda E, dti=dti: E.tensor_tensor(
                        out=u_f[:, dti * C:(dti + 1) * C],
                        in0=dt_f[:, dti * C:(dti + 1) * C],
                        in1=xn_f[:, dti * C:(dti + 1) * C], op=ALU.mult))
                    v_op(lambda E, dti=dti: E.tensor_scalar(
                        out=ya_f[:, dti * C:(dti + 1) * C],
                        in0=xn_f[:, dti * C:(dti + 1) * C],
                        scalar1=dpar_t[:, dti:dti + 1], scalar2=None, op0=ALU.mult))
                mv_u = v_mark()
                wv("scalar", mv_u)
                if DEBUG_STOP == "P2h" and k == 1:
                    tp.op("vector", lambda E: E.tensor_copy(
                        dbx_sl[:, C:2 * C], dtraw_b[:]))
                    mv_dbg = tp.mark("vector", "dv")
                    wv("gpsimd", mv_dbg)
                    gstore(out_flat[0:128 * 64].rearrange("(a b) -> a b", a=128),
                           hstate[:])
                    gstore(out_flat[128 * 64:128 * 64 + 128 * 4096].rearrange(
                        "(a b) -> a b", a=128), dt_f[:])
                    gstore(out_flat[128 * 64 + 128 * 4096:
                                    128 * 64 + 128 * 4096 + 128 * 1024].rearrange(
                        "(a b) -> a b", a=128), dbx_sl[:, C:2 * C])
                    tp.wait("gpsimd", "st", tp.cur("st"))
                    raise _StopBuild

                pend_copy = [None]
                for n in range(NST):
                    def rep(E, n=n):
                        for nb in range(2):
                            for i, src in enumerate((bc_h, bc_l, bc_l2)):
                                E.matmul(out=psA[:, nb * 512:(nb + 1) * 512],
                                         lhsT=sel32[:, n * 128:(n + 1) * 128],
                                         rhs=src[:, nb * 512:(nb + 1) * 512],
                                         start=(i == 0), stop=(i == 2))
                            for i, src in enumerate((bc_h, bc_l, bc_l2)):
                                last = E.matmul(out=psB[:, nb * 512:(nb + 1) * 512],
                                         lhsT=sel32[:, (16 + n) * 128:(17 + n) * 128],
                                         rhs=src[:, nb * 512:(nb + 1) * 512],
                                         start=(i == 0), stop=(i == 2))
                        return last
                    if n == 0:
                        wv("tensor", mv_u)
                    else:
                        wv("tensor", tp.cur("dv"))
                    mp_rep = p_mark(rep)
                    for dti in range(4):
                        git = git0 + n * 4 + dti
                        sl = git % 2
                        if git >= 2:
                            wv("scalar", da_gate[git - 2])
                        ma_da = s_mark(lambda E, dti=dti, n=n, sl=sl: E.activation(
                            out=da_sl[:, sl * C:(sl + 1) * C],
                            in_=dt_f[:, dti * C:(dti + 1) * C],
                            func=AF.Exp,
                            scale=aneg[:, dti * 16 + n:dti * 16 + n + 1]))
                        wa("vector", ma_da)
                        wp("vector", mp_rep)
                        if git >= 2:
                            wg("vector", git - 1)

                        def scan_body(E, dti=dti, n=n, sl=sl,
                                      prev=pend_copy[0]):
                            E.tensor_tensor(out=dbx_sl[:, sl * C:(sl + 1) * C],
                                            in0=u_f[:, dti * C:(dti + 1) * C],
                                            in1=psA[:, 0:1024], op=ALU.mult)
                            if prev is not None:
                                pd, pn, psl = prev
                                E.tensor_copy(
                                    hstate[:, pd * 16 + pn:pd * 16 + pn + 1],
                                    h_sl[:, (psl + 1) * C - 1:(psl + 1) * C])
                            return E.tensor_tensor_scan(
                                out=h_sl[:, sl * C:(sl + 1) * C],
                                data0=da_sl[:, sl * C:(sl + 1) * C],
                                data1=dbx_sl[:, sl * C:(sl + 1) * C],
                                initial=hstate[:, dti * 16 + n:dti * 16 + n + 1],
                                op0=ALU.mult, op1=ALU.add)
                        pend_copy[0] = (dti, n, sl)
                        mv_scan = v_mark(lambda E, f=scan_body, dti=dti, sl=sl: (
                            f(E),
                            E.tensor_tensor(
                                out=hc_sl[:, sl * C:(sl + 1) * C],
                                in0=h_sl[:, sl * C:(sl + 1) * C],
                                in1=psB[:, 0:1024], op=ALU.mult))[-1])
                        da_gate[git] = mv_scan
                        wv("gpsimd", mv_scan)
                        g_mark(lambda E, dti=dti, sl=sl: E.tensor_tensor(
                            out=ya_f[:, dti * C:(dti + 1) * C],
                            in0=ya_f[:, dti * C:(dti + 1) * C],
                            in1=hc_sl[:, sl * C:(sl + 1) * C], op=ALU.add))
                def flush_copy(E, prev=pend_copy[0]):
                    pd, pn, psl = prev
                    return E.tensor_copy(
                        hstate[:, pd * 16 + pn:pd * 16 + pn + 1],
                        h_sl[:, (psl + 1) * C - 1:(psl + 1) * C])
                v_mark(flush_copy)
                pend_copy[0] = None
                wg("tensor", git0 + NST * 4)
                for dti in range(4):
                    for g in range(2):
                        for j in range(4):
                            tt = g * 4 + j
                            p_op(lambda E, dti=dti, tt=tt, j=j: E.transpose(
                                out=psT[:, j * 128:(j + 1) * 128],
                                in_=ya_f[:, dti * C + tt * 128:dti * C + tt * 128 + 128],
                                identity=ident_f[:]))
                        mp = p_mark()
                        wp("scalar", mp)

                        def evac_y(E, dti=dti, g=g):
                            for j in range(4):
                                tt = g * 4 + j
                                last = E.activation(
                                    out=scr[:, tt * DQ + dti * 128:tt * DQ + dti * 128 + 128],
                                    in_=psT[:, j * 128:(j + 1) * 128], func=AF.Copy)
                            return last
                        ma = s_mark(evac_y)
                        wa("tensor", ma)
                ma_yT = tp.cur("ac")
                wa("vector", ma_yT)
                for tt in range(NTT):
                    v_op(lambda E, k=k, tt=tt: E.reduce_sum(
                        st1[:, k * 24 + tt:k * 24 + tt + 1],
                        scr[:, tt * DQ:(tt + 1) * DQ], axis=AX.X))
                    v_op(lambda E, k=k, tt=tt: E.tensor_tensor(
                        out=sm_scr[:, 0:512],
                        in0=scr[:, tt * DQ:(tt + 1) * DQ],
                        in1=xT_t[:, tt * DQ:(tt + 1) * DQ], op=ALU.mult))
                    v_op(lambda E, k=k, tt=tt: E.reduce_sum(
                        st1[:, k * 24 + 16 + tt:k * 24 + 17 + tt],
                        sm_scr[:, 0:512], axis=AX.X))
                mv_st = v_mark()
                wv("scalar", mv_st)
                for tt in range(NTT):
                    s_op(lambda E, k=k, tt=tt: E.activation(
                        out=sm_scr[:, 0:512],
                        in_=scr[:, tt * DQ:(tt + 1) * DQ], func=AF.Square,
                        accum_out=st1[:, k * 24 + 8 + tt:k * 24 + 9 + tt]))
                ma_st = s_mark()
                wa("gpsimd", ma_st)
                wv("gpsimd", mv_st)
                gstore(yb_chunks[k], scr_v)
            wv("gpsimd", tp.cur("dv"))
            wa("gpsimd", tp.cur("ac"))
            gstore(cc1_in.ap(), st1[:])
            wst("gpsimd")
            cc_s1 = collective("AllReduce", ALU.add, cc1_in, cc1_out)
            if DEBUG_STOP == "P2":
                tp.wait("gpsimd", "cc", cc_s1)
                maybe_stop("P2", [ybuf])
            wc("sync", cc_s1)
            sload(st1[:], cc1_out.ap())

            for k in range(NCH):
                tp.wait("sync", "dv", tp.cur("dv"))
                tp.wait("sync", "ac", tp.cur("ac"))
                tp.wait("sync", "st", tp.cur("st"))
                sload(xT_v, xT_chunks[k])
                sload(scr_v, yb_chunks[k])
                wl("vector")

                mu1 = lncol[:, 0:8]
                var1 = lncol[:, 8:16]
                rr = lncol[:, 16:24]
                m1 = lncol[:, 24:32]
                t1 = lncol[:, 32:40]
                t2 = lncol[:, 40:48]
                mv = vseq(
                    lambda E, k=k: E.tensor_scalar_mul(
                        mu1, st1[:, k * 24:k * 24 + 8], 1.0 / D),
                    lambda E, k=k: E.tensor_scalar_mul(
                        var1, st1[:, k * 24 + 8:k * 24 + 16], 1.0 / D),
                    lambda E: E.tensor_tensor(out=lncol[:, 56:64], in0=mu1,
                                              in1=mu1, op=ALU.mult),
                    lambda E: E.tensor_tensor(out=var1, in0=var1,
                                              in1=lncol[:, 56:64], op=ALU.subtract),
                    lambda E: E.tensor_copy(rr, var1),
                    lambda E: E.tensor_scalar_add(rr, rr, EPS),
                )
                wv("scalar", mv)
                ma = s_mark(lambda E: E.activation(out=lncol[:, 16:24],
                                                   in_=lncol[:, 16:24], func=AF.Sqrt))
                wa("vector", ma)
                mv = vseq(
                    lambda E: E.reciprocal(rr, rr),
                    lambda E: E.tensor_tensor(out=m1, in0=mu1, in1=rr, op=ALU.mult),
                    lambda E: E.tensor_scalar_mul(m1, m1, -1.0),
                    lambda E, k=k: E.tensor_tensor(
                        out=t1, in0=mu1, in1=st1[:, k * 24:k * 24 + 8], op=ALU.mult),
                    lambda E, k=k: E.tensor_tensor(
                        out=t1, in0=st1[:, k * 24 + 16:k * 24 + 24], in1=t1,
                        op=ALU.subtract),
                    lambda E: E.tensor_tensor(out=t1, in0=t1, in1=rr, op=ALU.mult),
                    lambda E: E.tensor_scalar_mul(t1, t1, 2.0),
                    lambda E: E.tensor_tensor(out=t2, in0=var1, in1=rr, op=ALU.mult),
                    lambda E: E.tensor_tensor(out=t2, in0=t2, in1=rr, op=ALU.mult),
                    lambda E: E.tensor_scalar_mul(t2, t2, float(D)),
                    lambda E: E.tensor_tensor(out=t1, in0=t1, in1=t2, op=ALU.add),
                    lambda E, k=k: E.tensor_tensor(
                        out=t1, in0=st0[:, k * 16 + 8:k * 16 + 16], in1=t1,
                        op=ALU.add),
                    lambda E: E.tensor_scalar_mul(t1, t1, 1.0 / D),
                    lambda E, k=k: E.tensor_tensor(
                        out=t2, in0=mu0[:, k * 8:(k + 1) * 8],
                        in1=mu0[:, k * 8:(k + 1) * 8], op=ALU.mult),
                    lambda E: E.tensor_tensor(out=t1, in0=t1, in1=t2,
                                              op=ALU.subtract),
                    lambda E: E.tensor_scalar_add(t1, t1, EPS),
                )
                wv("scalar", mv)
                ma = s_mark(lambda E: E.activation(out=lncol[:, 32:40],
                                                   in_=lncol[:, 32:40], func=AF.Sqrt))
                wa("vector", ma)
                vseq(
                    lambda E: E.reciprocal(t1, t1),
                    lambda E, k=k: E.tensor_tensor(
                        out=t2, in0=mu0[:, k * 8:(k + 1) * 8], in1=t1, op=ALU.mult),
                    lambda E: E.tensor_scalar_mul(t2, t2, -1.0),
                )
                for tt in range(NTT):
                    def x1_body(E, k=k, tt=tt):
                        seg_y = scr[:, tt * DQ:(tt + 1) * DQ]
                        seg_x = xT_t[:, tt * DQ:(tt + 1) * DQ]
                        E.tensor_scalar(out=seg_y, in0=seg_y,
                                        scalar1=lncol[:, 16 + tt:17 + tt],
                                        scalar2=lncol[:, 24 + tt:25 + tt],
                                        op0=ALU.mult, op1=ALU.add)
                        E.tensor_tensor(out=seg_y, in0=seg_y, in1=seg_x, op=ALU.add)
                        return E.tensor_scalar(
                            out=u_f[:, tt * DQ:(tt + 1) * DQ], in0=seg_y,
                            scalar1=lncol[:, 32 + tt:33 + tt],
                            scalar2=lncol[:, 40 + tt:41 + tt],
                            op0=ALU.mult, op1=ALU.add)
                    v_mark(x1_body)
                mv_x1 = tp.cur("dv")
                if DEBUG_STOP == "P3s" and k == 0:
                    wv("gpsimd", mv_x1)
                    gstore(out_flat[0:128 * 64].rearrange("(a b) -> a b", a=128),
                           lncol[:])
                    gstore(out_flat[128 * 64:128 * 64 + 128 * 4096].rearrange(
                        "(a b) -> a b", a=128), scr[:])
                    gstore(out_flat[128 * 64 + 128 * 4096:
                                    128 * 64 + 128 * 4096 + 128 * 24 * NCH].rearrange(
                        "(a b) -> a b", a=128), st1[:])
                    tp.wait("gpsimd", "st", tp.cur("st"))
                    raise _StopBuild
                wv("gpsimd", mv_x1)
                gstore(x1_chunks[k], scr_v)
                wv("tensor", mv_x1)
                for dti in range(4):
                    for g in range(2):
                        for j in range(4):
                            tt = g * 4 + j
                            p_op(lambda E, dti=dti, tt=tt, j=j: E.transpose(
                                out=psT[:, j * 128:(j + 1) * 128],
                                in_=u_f[:, tt * DQ + dti * 128:tt * DQ + dti * 128 + 128],
                                identity=ident_f[:]))
                        mp = p_mark()
                        wp("scalar", mp)
                        ma = s_mark(lambda E, dti=dti, g=g: E.activation(
                            out=up_b[:, dti * C + g * 512:dti * C + g * 512 + 512],
                            in_=psT[:, 0:512], func=AF.Copy))
                        wa("tensor", ma)
                    wa("gpsimd", tp.cur("ac"))
                    gstore(agx_in.ap()[dti * 128:(dti + 1) * 128, k * C:(k + 1) * C],
                           up_b[:, dti * C:(dti + 1) * C])
            wst("gpsimd")
            cc_agx = collective("AllGather", ALU.bypass, agx_in, agx_out)
            if DEBUG_STOP == "P3":
                tp.wait("gpsimd", "cc", cc_agx)
                maybe_stop("P3", [x1buf])

            ACC8 = [[(psA, 0), (psA, 512)], [(psB, 0), (psB, 512)],
                    [(psM, 0), (psN, 0)], [(psT, 0), (psF, 0)]]
            mm_hist = []
            tp.wait("sync", "cc", cc_agx)
            for k in range(NCH):
                for kt in range(16):
                    sl = kt % 4
                    if len(mm_hist) >= 4:
                        wp("sync", mm_hist[-4])
                    lv = sload(x1n_b[:, sl * C:(sl + 1) * C],
                               agx_out.ap()[kt * 128:(kt + 1) * 128,
                                            k * C:(k + 1) * C])

                    def mm_up(E, kt=kt, sl=sl):
                        for mt in range(4):
                            for nh in range(2):
                                ps, off = ACC8[mt][nh]
                                last = E.matmul(
                                    out=ps[:, off:off + 512],
                                    lhsT=wup_bf[:, kt * DQ + mt * 128:kt * DQ + mt * 128 + 128],
                                    rhs=x1n_b[:, sl * C + nh * 512:sl * C + nh * 512 + 512],
                                    start=(kt == 0), stop=(kt == 15))
                        return last
                    tp.wait("tensor", "ld", lv)
                    mm_hist.append(p_mark(mm_up))
                wp("scalar", mm_hist[-1])
                wst("scalar")

                def evac_up(E):
                    E.activation(out=up_b[:, 0:C], in_=psA[:, 0:1024], func=AF.Relu)
                    E.activation(out=up_b[:, C:2 * C], in_=psB[:, 0:1024], func=AF.Relu)
                    E.activation(out=up_b[:, 2 * C:2 * C + 512], in_=psM[:, 0:512],
                                 func=AF.Relu)
                    E.activation(out=up_b[:, 2 * C + 512:3 * C], in_=psN[:, 0:512],
                                 func=AF.Relu)
                    E.activation(out=up_b[:, 3 * C:3 * C + 512], in_=psT[:, 0:512],
                                 func=AF.Relu)
                    return E.activation(out=up_b[:, 3 * C + 512:4 * C], in_=psF[:, 0:512],
                                        func=AF.Relu)
                ma = s_mark(evac_up)
                wa("tensor", ma)
                wa("gpsimd", ma)
                for mt in range(4):
                    gstore(agu_in.ap()[mt * 128:(mt + 1) * 128, k * C:(k + 1) * C],
                           up_b[:, mt * C:(mt + 1) * C])
            wst("gpsimd")
            cc_agu = collective("AllGather", ALU.bypass, agu_in, agu_out)
            if DEBUG_STOP == "P4":
                tp.wait("gpsimd", "cc", cc_agu)
                maybe_stop("P4", [agu_in])

            p4_last_mm = mm_hist[-1]
            mm_hist = []
            tp.wait("sync", "cc", cc_agu)
            wp("sync", p4_last_mm)
            for k in range(NCH):
                for kt in range(16):
                    sl = kt % 4
                    if len(mm_hist) >= 4:
                        wp("sync", mm_hist[-4])
                    sload(x1n_b[:, sl * C:(sl + 1) * C],
                          agu_out.ap()[kt * 128:(kt + 1) * 128, k * C:(k + 1) * C])
                    lv2 = sload(up_b[:, sl * C:(sl + 1) * C],
                                agx_out.ap()[kt * 128:(kt + 1) * 128, k * C:(k + 1) * C])

                    def mm_f(E, kt=kt, sl=sl):
                        for mt in range(4):
                            for nh in range(2):
                                ps, off = ACC8[mt][nh]
                                E.matmul(
                                    out=ps[:, off:off + 512],
                                    lhsT=wdn_bf[:, kt * DQ + mt * 128:kt * DQ + mt * 128 + 128],
                                    rhs=x1n_b[:, sl * C + nh * 512:sl * C + nh * 512 + 512],
                                    start=(kt == 0), stop=False)
                                last = E.matmul(
                                    out=ps[:, off:off + 512],
                                    lhsT=wex_bf[:, kt * DQ + mt * 128:kt * DQ + mt * 128 + 128],
                                    rhs=up_b[:, sl * C + nh * 512:sl * C + nh * 512 + 512],
                                    start=False, stop=(kt == 15))
                        return last
                    tp.wait("tensor", "ld", lv2)
                    mm_hist.append(p_mark(mm_f))
                wp("vector", mm_hist[-1])

                def evac_f(E):
                    E.tensor_copy(u_f[:, 0:C], psA[:, 0:1024])
                    E.tensor_copy(u_f[:, C:2 * C], psB[:, 0:1024])
                    E.tensor_copy(u_f[:, 2 * C:2 * C + 512], psM[:, 0:512])
                    E.tensor_copy(u_f[:, 2 * C + 512:3 * C], psN[:, 0:512])
                    E.tensor_copy(u_f[:, 3 * C:3 * C + 512], psT[:, 0:512])
                    return E.tensor_copy(u_f[:, 3 * C + 512:4 * C], psF[:, 0:512])
                mv = v_mark(evac_f)
                wv("tensor", mv)
                mv_f = tp.cur("dv")
                wv("tensor", mv_f)
                wst("tensor")
                for dti in range(4):
                    for g in range(2):
                        for j in range(4):
                            tt = g * 4 + j
                            p_op(lambda E, dti=dti, tt=tt, j=j: E.transpose(
                                out=psT[:, j * 128:(j + 1) * 128],
                                in_=u_f[:, dti * C + tt * 128:dti * C + tt * 128 + 128],
                                identity=ident_f[:]))
                        mp = p_mark()
                        wp("scalar", mp)

                        def evac_fT(E, dti=dti, g=g):
                            for j in range(4):
                                tt = g * 4 + j
                                last = E.activation(
                                    out=scr[:, tt * DQ + dti * 128:tt * DQ + dti * 128 + 128],
                                    in_=psT[:, j * 128:(j + 1) * 128], func=AF.Copy)
                            return last
                        ma = s_mark(evac_fT)
                        wa("tensor", ma)
                ma_fT = tp.cur("ac")
                wa("vector", ma_fT)
                for tt in range(NTT):
                    v_op(lambda E, k=k, tt=tt: E.reduce_sum(
                        st3[:, k * 16 + tt:k * 16 + tt + 1],
                        scr[:, tt * DQ:(tt + 1) * DQ], axis=AX.X))
                mv_s3 = v_mark()
                wv("scalar", mv_s3)
                for tt in range(NTT):
                    s_op(lambda E, k=k, tt=tt: E.activation(
                        out=sm_scr[:, 0:512],
                        in_=scr[:, tt * DQ:(tt + 1) * DQ], func=AF.Square,
                        accum_out=st3[:, k * 16 + 8 + tt:k * 16 + 9 + tt]))
                ma_s3 = s_mark()
                wa("gpsimd", ma_s3)
                wv("gpsimd", mv_s3)
                gstore(fb_chunks[k], scr_v)
            wv("gpsimd", tp.cur("dv"))
            wa("gpsimd", tp.cur("ac"))
            gstore(cc3_in.ap(), st3[:])
            wst("gpsimd")
            cc_s3 = collective("AllReduce", ALU.add, cc3_in, cc3_out)
            if DEBUG_STOP == "P5a":
                tp.wait("gpsimd", "cc", cc_s3)
                maybe_stop("P5a", [fbuf])
            wc("sync", cc_s3)
            sload(st3[:], cc3_out.ap())

            for k in range(NCH):
                tp.wait("sync", "dv", tp.cur("dv"))
                tp.wait("sync", "st", tp.cur("st"))
                sload(xT_v, x1_chunks[k])
                sload(scr_v, fb_chunks[k])
                wl("vector")

                mv = vseq(
                    lambda E, k=k: E.tensor_scalar_mul(
                        lncol[:, 0:8], st3[:, k * 16:k * 16 + 8], 1.0 / D),
                    lambda E, k=k: E.tensor_scalar_mul(
                        lncol[:, 8:16], st3[:, k * 16 + 8:k * 16 + 16], 1.0 / D),
                    lambda E: E.tensor_tensor(out=lncol[:, 56:64], in0=lncol[:, 0:8],
                                              in1=lncol[:, 0:8], op=ALU.mult),
                    lambda E: E.tensor_tensor(out=lncol[:, 8:16], in0=lncol[:, 8:16],
                                              in1=lncol[:, 56:64], op=ALU.subtract),
                    lambda E: E.tensor_copy(lncol[:, 16:24], lncol[:, 8:16]),
                    lambda E: E.tensor_scalar_add(lncol[:, 16:24], lncol[:, 16:24],
                                                  EPS),
                )
                wv("scalar", mv)
                ma = s_mark(lambda E: E.activation(out=lncol[:, 16:24],
                                                   in_=lncol[:, 16:24], func=AF.Sqrt))
                wa("vector", ma)
                vseq(
                    lambda E: E.reciprocal(lncol[:, 16:24], lncol[:, 16:24]),
                    lambda E: E.tensor_tensor(out=lncol[:, 24:32], in0=lncol[:, 0:8],
                                              in1=lncol[:, 16:24], op=ALU.mult),
                    lambda E: E.tensor_scalar_mul(lncol[:, 24:32], lncol[:, 24:32],
                                                  -1.0),
                )

                def outb(E, k=k):
                    rr = lncol[:, 16:24]
                    m3 = lncol[:, 24:32]
                    for tt in range(NTT):
                        seg_f = scr[:, tt * DQ:(tt + 1) * DQ]
                        E.tensor_scalar(out=seg_f, in0=seg_f,
                                        scalar1=rr[:, tt:tt + 1],
                                        scalar2=m3[:, tt:tt + 1],
                                        op0=ALU.mult, op1=ALU.add)
                        last = E.tensor_tensor(out=seg_f, in0=seg_f,
                                        in1=xT_t[:, tt * DQ:(tt + 1) * DQ], op=ALU.add)
                    return last
                mv = v_mark(outb)
                wv("gpsimd", mv)
                gstore(out_hbm.rearrange("(k a p) d -> k p a d", p=128, a=NTT)[k],
                       scr_v)
            final_st = tp.cur("st")
            tp.wait("gpsimd", "st", final_st)


        try:
            emit()
        except _StopBuild:
            pass

        @block.sync
        def _(E):
            tp.replay("sync", E, SEMS)

        @block.gpsimd
        def _(E):
            tp.replay("gpsimd", E, SEMS)

        @block.vector
        def _(E):
            tp.replay("vector", E, SEMS)

        @block.scalar
        def _(E):
            tp.replay("scalar", E, SEMS)

        @block.tensor
        def _(E):
            tp.replay("tensor", E, SEMS)

    return nc


def _per_core_inputs(inputs, core):
    g, q = core // 4, core % 4
    dq = slice(DQ * q, DQ * (q + 1))
    x = np.asarray(inputs["L0_attn_x"])
    W_x = np.asarray(inputs["L0_attn_W_x"])
    W_dt = np.asarray(inputs["L0_attn_W_dt"])
    b_dt = np.asarray(inputs["L0_attn_b_dt"])
    A_log = np.asarray(inputs["L0_attn_A_log"])
    D_param = np.asarray(inputs["L0_attn_D_param"])
    W_up = np.asarray(inputs["L0_ffn_W_up"])
    W_down = np.asarray(inputs["L0_ffn_W_down"])
    W_expert = np.asarray(inputs["L0_ffn_W_expert"])

    sel = np.zeros((32, 32 * 128), np.float32)
    for n in range(32):
        sel[n, n * 128:(n + 1) * 128] = 1.0

    return {
        "xT_in": np.ascontiguousarray(x[g][:, dq]),
        "wx_in": np.ascontiguousarray(W_x.T[dq, :]),
        "wdt_in": np.ascontiguousarray(W_dt[dq, :].T),
        "bdt_in": np.ascontiguousarray(b_dt[dq].reshape(4, 128).T),
        "alog_in": np.ascontiguousarray(
            A_log[dq].reshape(4, 128, NST).transpose(1, 0, 2).reshape(128, 64)),
        "dpar_in": np.ascontiguousarray(D_param[dq].reshape(4, 128).T),
        "wup_in": np.ascontiguousarray(W_up[dq, :].T),
        "wdn_in": np.ascontiguousarray(W_down[dq, :].T),
        "wex_in": np.ascontiguousarray(W_expert[:, dq]),
        "ident_in": np.eye(128, dtype=np.float32),
        "sel32_in": sel,
    }


def kernel(**inputs):
    nc = build()
    in_maps = [_per_core_inputs(inputs, c) for c in range(8)]
    res = run_bass_kernel_spmd(nc, in_maps, core_ids=list(range(8)))
    out = np.empty((B, T, D), np.float32)
    for c in range(8):
        g, q = c // 4, c % 4
        out[g][:, DQ * q:DQ * (q + 1)] = res.results[c]["outT"]
    return out


if __name__ == "__main__":
    nc = build()
    print("build ok; instrs:",
          sum(1 for _ in nc.all_instructions()) if hasattr(nc, "all_instructions") else "?")
